# version 15
# baseline (speedup 1.0000x reference)
import numpy as np
from contextlib import ExitStack

import concourse.bass as bass
import concourse.mybir as mybir
from concourse.bass_utils import run_bass_kernel_spmd

F32 = mybir.dt.float32
BF16 = mybir.dt.bfloat16
F8 = mybir.dt.float8e4
U16 = mybir.dt.uint16
AX = mybir.AxisListType
ALU = mybir.AluOpType
ACTF = mybir.ActivationFunctionType

NCORE = 8
TOK = 1024
NB = 2
D = 2048
NDC = D // 128
QKV = 1536
NOC = QKV // 128
W = 512
NWC = W // 128
NPAIR = 4
EPS_LN = 1e-5
ENG_SEM = {"v": "V", "g": "G", "a": "A", "t": "T"}


class Bld:
    def __init__(self, nc):
        self.nc = nc
        self.q = {e: [] for e in "tvags"}
        self.sems = {}
        self.cnt = {}
        self.hwm = {e: {} for e in "tvags"}

    def mksem(self, stk, name):
        self.sems[name] = stk.enter_context(self.nc.semaphore(name))
        self.cnt[name] = 0

    def ev(self, name):
        return (name, self.cnt[name])

    def op(self, eng, fn, waits=(), inc=None, amt=1):
        if eng in "vag":
            waits = list(waits)
            own = ENG_SEM[eng]
            if self.cnt[own] > 0:
                waits.append((own, self.cnt[own]))
            if inc is None:
                inc, amt = own, 1
        wl = []
        for wv in waits:
            if wv is None:
                continue
            s, v = wv
            if v <= 0:
                continue
            h = self.hwm[eng]
            if h.get(s, 0) >= v:
                continue
            h[s] = v
            wl.append((s, v))
        if inc is not None:
            self.cnt[inc] += amt
        sems = self.sems

        def run(E, wl=wl, fn=fn, inc=inc, amt=amt):
            for s, v in wl:
                E.wait_ge(sems[s], v)
            ins = fn(E)
            if inc is not None:
                ins.then_inc(sems[inc], amt)
        self.q[eng].append(run)
        return (inc, self.cnt[inc]) if inc is not None else None

    def dma(self, qeng, sem, out, in_, waits=(), **kw):
        return self.op(qeng, lambda E: E.dma_start(out=out, in_=in_, **kw),
                       waits, inc=sem, amt=16)

    def dma_evs(self):
        return [self.ev(s) for s in self.cnt if s.startswith("d_")]

    def barrier_evs(self):
        return [self.ev(s) for s in ("T", "V", "A", "G")]


class Rot:
    def __init__(self, items):
        self.items = list(items)
        self.rel = [None] * len(self.items)
        self.i = 0

    def take(self):
        s = self.i % len(self.items)
        self.i += 1
        return s, self.items[s], self.rel[s]

    def release(self, s, ev):
        self.rel[s] = ev


def build_program():
    nc = bass.Bass("TRN2", target_bir_lowering=False)

    x_h = nc.dram_tensor("x", [TOK + 2, D], F32, kind="ExternalInput").ap()
    cw_h = nc.dram_tensor("conv_w", [3, D, QKV], F32, kind="ExternalInput").ap()
    cb_h = nc.dram_tensor("conv_b", [QKV], F32, kind="ExternalInput").ap()
    wq_h = nc.dram_tensor("wq", [W, W], F32, kind="ExternalInput").ap()
    wk_h = nc.dram_tensor("wk", [W, W], F32, kind="ExternalInput").ap()
    wv_h = nc.dram_tensor("wv", [W, W], F32, kind="ExternalInput").ap()
    bq_h = nc.dram_tensor("bq", [W], F32, kind="ExternalInput").ap()
    bk_h = nc.dram_tensor("bk", [W], F32, kind="ExternalInput").ap()
    bv_h = nc.dram_tensor("bv", [W], F32, kind="ExternalInput").ap()
    wo_h = nc.dram_tensor("wo", [W, W], F32, kind="ExternalInput").ap()
    bo_h = nc.dram_tensor("bo", [W], F32, kind="ExternalInput").ap()
    pw_h = nc.dram_tensor("proj_w", [W, D], F32, kind="ExternalInput").ap()
    pb_h = nc.dram_tensor("proj_b", [D], F32, kind="ExternalInput").ap()
    fw_h = nc.dram_tensor("ffn_w", [D, D], BF16, kind="ExternalInput").ap()
    fb_h = nc.dram_tensor("ffn_b", [D], BF16, kind="ExternalInput").ap()
    out_h = nc.dram_tensor("out", [D, TOK], F32, kind="ExternalOutput").ap()

    xh_scr = nc.dram_tensor("xh_scr", [128, NDC * (TOK + 2)], BF16).ap()
    xl_scr = nc.dram_tensor("xl_scr", [128, NDC * (TOK + 2)], BF16).ap()
    qkvh_scr = nc.dram_tensor("qkvh_scr", [NOC, 128, TOK], BF16).ap()
    qkvl_scr = nc.dram_tensor("qkvl_scr", [NOC, 128, TOK], BF16).ap()
    x1_scr = nc.dram_tensor("x1_scr", [NDC, 128, TOK], F32).ap()
    cc_in = nc.dram_tensor("cc_in", [1, 8], F32)
    cc_out = nc.dram_tensor("cc_out", [1, 8], F32)

    b = Bld(nc)

    with ExitStack() as top:
        for s in (["T", "V", "A", "G", "C"]
                  + ["d_bias", "d_xt0", "d_xt1", "d_cw0", "d_cw1", "d_cw2",
                     "d_qd0", "d_qd1", "d_qd2", "d_qd3",
                     "d_ql0", "d_ql1", "d_ql2", "d_ql3",
                     "d_qin", "d_kin", "d_vin", "d_wsf", "d_pwf0", "d_pwf1",
                     "d_xr0", "d_xr1", "d_xr2", "d_xr3",
                     "d_x1o", "d_x1r", "d_wt0", "d_wt1",
                     "d_ot0", "d_ot1", "d_ot2", "d_ot3",
                     "d_ccd", "d_xscr", "d_fb", "d_wA0", "d_wA1", "d_xs0", "d_xs1"]):
            b.mksem(top, s)
        ec = top.enter_context

        PF = [ec(nc.psum_tensor(f"pf{i}", [128, 512], F32)) for i in range(4)]
        PT0 = ec(nc.psum_tensor("pt0", [128, 1024], BF16))
        PT1 = ec(nc.psum_tensor("pt1", [128, 1024], BF16))
        PS0 = ec(nc.psum_tensor("ps0", [1, 512], F32))
        PS1 = ec(nc.psum_tensor("ps1", [1, 512], F32))
        pf_rot = Rot(PF)
        pt_rel = [None, None]
        ps_rel = [None]

        idn = ec(nc.sbuf_tensor("idn", [128, 128], F32))
        idnb = ec(nc.sbuf_tensor("idnb", [128, 128], BF16))
        ones_col = ec(nc.sbuf_tensor("ones_col", [128, 1], F32))
        ones_row = ec(nc.sbuf_tensor("ones_row", [1, 128], F32))
        amask = [ec(nc.sbuf_tensor(f"amask{i}", [128, 512], BF16)) for i in range(4)]
        cb_s = ec(nc.sbuf_tensor("cb_s", [128, NOC], F32))
        bqs_s = ec(nc.sbuf_tensor("bqs_s", [128, NWC], F32))
        bq_s = ec(nc.sbuf_tensor("bq_s", [128, NWC], F32))
        bk_s = ec(nc.sbuf_tensor("bk_s", [128, NWC], F32))
        bo_s = ec(nc.sbuf_tensor("bo_s", [128, NWC], F32))
        pb_s = ec(nc.sbuf_tensor("pb_s", [128, NDC], F32))
        fb_s = ec(nc.sbuf_tensor("fb_s", [128, NDC], F32))
        bv_row = ec(nc.sbuf_tensor("bv_row", [1, 512], F32))
        bv_bc = ec(nc.sbuf_tensor("bv_bc", [128, 512], F32))
        x1s = [ec(nc.sbuf_tensor(f"x1s{i}", [128, 512], F32)) for i in range(NDC)]
        sqr = Rot([ec(nc.sbuf_tensor(f"sq{i}", [128, 512], F32)) for i in range(2)])
        m_row = ec(nc.sbuf_tensor("m_row", [1, 512], F32))
        v_row = ec(nc.sbuf_tensor("v_row", [1, 512], F32))
        i_row = ec(nc.sbuf_tensor("i_row", [1, 512], F32))
        t_row = ec(nc.sbuf_tensor("t_row", [1, 512], F32))
        u_row = ec(nc.sbuf_tensor("u_row", [1, 512], F32))
        chmax = ec(nc.sbuf_tensor("chmax", [128, NDC], F32))
        wmax = ec(nc.sbuf_tensor("wmax", [128, NDC], F32))
        red1 = ec(nc.sbuf_tensor("red1", [128, 1], F32))
        red2 = ec(nc.sbuf_tensor("red2", [128, 1], F32))
        cc_s = ec(nc.sbuf_tensor("cc_s", [1, 8], F32))
        xs_t = ec(nc.sbuf_tensor("xs_t", [1, 1], F32))
        ws_t = ec(nc.sbuf_tensor("ws_t", [1, 1], F32))
        xs_bc = ec(nc.sbuf_tensor("xs_bc", [128, 1], F32))
        ws_bc = ec(nc.sbuf_tensor("ws_bc", [128, 1], F32))
        rxs = ec(nc.sbuf_tensor("rxs", [128, 1], F32))
        rws = ec(nc.sbuf_tensor("rws", [128, 1], F32))
        sy_bc = ec(nc.sbuf_tensor("sy_bc", [128, 1], F32))
        m_bc_s = ec(nc.sbuf_tensor("m_bc_s", [128, 512], F32))
        i_bc_s = ec(nc.sbuf_tensor("i_bc_s", [128, 512], F32))
        nmx_a = ec(nc.sbuf_tensor("nmx_a", [128, 16], F32))
        sum_a = ec(nc.sbuf_tensor("sum_a", [128, 16], F32))
        rec_a = ec(nc.sbuf_tensor("rec_a", [128, 16], F32))

        b.op("g", lambda E: E.memset(ones_col[:], 1.0))
        b.op("g", lambda E: E.memset(ones_row[:], 1.0))
        b.op("g", lambda E: E.memset(cc_s[:], 0.0))

        b.op("g", lambda E: E.memset(idn[:], 1.0))
        b.op("g", lambda E: E.affine_select(
            out=idn[:], in_=idn[:], compare_op=ALU.is_ge, fill=0.0, base=0,
            pattern=[[-1, 128]], channel_multiplier=1))
        e_id = b.op("g", lambda E: E.affine_select(
            out=idn[:], in_=idn[:], compare_op=ALU.is_ge, fill=0.0, base=0,
            pattern=[[1, 128]], channel_multiplier=-1))
        for qt in range(4):
            b.op("g", lambda E, qt=qt: E.memset(amask[qt][:], 0.0))
            b.op("g", lambda E, qt=qt: E.affine_select(
                out=amask[qt][:], in_=amask[qt][:], compare_op=ALU.is_ge,
                fill=-1e9, base=qt * 128, pattern=[[-1, 512]],
                channel_multiplier=1))
        e_masks = b.ev("G")
        e_idnb = b.op("v", lambda E: E.tensor_copy(idnb[:], idn[:]),
                      waits=[e_id], inc="V")

        def strided_bias(dst, src_ap, n):
            return b.dma("s", "d_bias", dst[:, 0:n],
                         bass.AP(src_ap.tensor, 0, [[1, 128], [128, n]]),
                         allow_slow_non_contiguous=True)
        strided_bias(cb_s, cb_h, NOC)
        strided_bias(bq_s, bq_h, NWC)
        strided_bias(bk_s, bk_h, NWC)
        strided_bias(bo_s, bo_h, NWC)
        strided_bias(pb_s, pb_h, NDC)
        b.dma("s", "d_bias", bv_row[:, :],
              bass.AP(bv_h.tensor, 0, [[1, 1], [1, 512]]))
        b.dma("g", "d_fb", fb_s[:, 0:NDC],
              bass.AP(fb_h.tensor, 0, [[1, 128], [128, NDC]]),
              allow_slow_non_contiguous=True)
        e_bias_dma = b.ev("d_bias")
        e_bv = e_bias_dma
        e_fb = b.ev("d_fb")
        e_bqs = b.op("v", lambda E: E.tensor_scalar_mul(bqs_s[:], bq_s[:], 0.125),
                     waits=[e_bias_dma], inc="V")
        sl0, bank0, rel0 = pf_rot.take()
        e_t0 = b.op("t", lambda E, p=bank0: E.matmul(
            p[:, 0:512], ones_row[0:1, :], bv_row[0:1, :], start=True, stop=True,
            skip_group_check=True), waits=[e_bv, rel0, e_masks], inc="T")
        e_bvbc = b.op("v", lambda E, p=bank0: E.tensor_copy(bv_bc[:], p[:, 0:512]),
                      waits=[e_t0], inc="V")
        pf_rot.release(sl0, e_bvbc)

        stkA = ExitStack()
        ecA = stkA.enter_context
        xtm = Rot([ecA(nc.sbuf_tensor(f"xtm{i}", [128, 2048], F32))
                   for i in range(2)])
        XW = TOK + 2
        xhb = ecA(nc.sbuf_tensor("xhb", [128, NDC * XW], BF16))
        xlb = ecA(nc.sbuf_tensor("xlb", [128, NDC * XW], BF16))

        def xh(dc, a, n):
            return xhb[:, dc * XW + a: dc * XW + a + n]

        def xl(dc, a, n):
            return xlb[:, dc * XW + a: dc * XW + a + n]
        NCW = 3
        cwf = Rot([ecA(nc.sbuf_tensor(f"cwf{i}", [128, 1536], F32))
                   for i in range(NCW)])
        cwh = [ecA(nc.sbuf_tensor(f"cwh{i}", [128, 1536], BF16))
               for i in range(NCW)]
        cwl = [ecA(nc.sbuf_tensor(f"cwl{i}", [128, 1536], BF16))
               for i in range(NCW)]
        cw_rel = [None] * NCW
        qkv_tmp = Rot([ecA(nc.sbuf_tensor(f"qtmp{i}", [128, 512], F32))
                       for i in range(2)])
        wt_stA = Rot([ecA(nc.sbuf_tensor(f"wtsa{i}", [128, 2048], BF16))
                      for i in range(2)])
        qkv_hi = Rot([ecA(nc.sbuf_tensor(f"qhi{i}", [128, 512], BF16))
                      for i in range(4)])
        qkv_lo = Rot([ecA(nc.sbuf_tensor(f"qlo{i}", [128, 512], BF16))
                      for i in range(4)])

        NGRP = NOC * 4
        gload = [None] * NGRP

        def load_group(k):
            oc, dcg = divmod(k, 4)
            csl, cbuf, crel = cwf.take()
            e_w = None
            for tap in range(3):
                src = bass.AP(cw_h.tensor,
                              tap * D * QKV + dcg * 4 * 128 * QKV + oc * 128,
                              [[QKV, 128], [128 * QKV, 4], [1, 128]])
                e_w = b.dma("s", f"d_cw{csl}",
                            cbuf[:, tap * 512:(tap + 1) * 512], src,
                            waits=[crel])
            e_sh = b.op("a", lambda E, cb_=cbuf, csl=csl: E.copy(
                cwh[csl][:], cb_[:]), waits=[e_w, cw_rel[csl]], inc="A")
            e_sl = b.op("v", lambda E, cb_=cbuf, csl=csl: E.tensor_tensor(
                cwl[csl][:], cb_[:], cwh[csl][:], ALU.subtract),
                waits=[e_w, e_sh])
            cwf.release(csl, e_sl)
            gload[k] = (csl, e_sh, e_sl)

        xdone_ev = [None] * NDC
        for t in range(9):
            rows = 2 if t == 8 else 128
            sl, xbuf, rel = xtm.take()
            e_ld = b.dma("s", f"d_xt{sl}", xbuf[0:rows, :],
                         x_h[t * 128:t * 128 + rows, :], waits=[rel])
            last = None
            col = t * 128
            for q4 in range(4):
                psl, pbank, prel = pf_rot.take()

                def tp4(E, p=pbank, xb=xbuf, q4=q4):
                    ins = None
                    for j in range(4):
                        dc = q4 * 4 + j
                        ins = E.transpose(p[:, j * 128:(j + 1) * 128],
                                          xb[:, dc * 128:(dc + 1) * 128],
                                          idn[:])
                    return ins
                e_tp = b.op("t", tp4, waits=[e_ld, prel, e_id], inc="T")
                dsth = bass.AP(xhb, (q4 * 4) * XW + col,
                               [[NDC * XW, 128], [XW, 4], [1, rows]])
                dstl = bass.AP(xlb, (q4 * 4) * XW + col,
                               [[NDC * XW, 128], [XW, 4], [1, rows]])
                srcp = bass.AP(pbank, 0, [[512, 128], [128, 4], [1, rows]])
                e_hi = b.op("a", lambda E, s=srcp, d=dsth: E.copy(d, s),
                            waits=[e_tp], inc="A")
                e_lo = b.op("v", lambda E, s=srcp, dh=dsth, dl=dstl:
                            E.tensor_tensor(dl, s, dh, ALU.subtract),
                            waits=[e_tp, e_hi])
                pf_rot.release(psl, e_lo)
                for j in range(4):
                    xdone_ev[q4 * 4 + j] = e_lo
                last = e_lo
            xtm.release(sl, last)
            if t < NCW:
                load_group(t)
        for h in range(4):
            cols = slice(h * 4 * XW, (h + 1) * 4 * XW)
            b.dma("g", "d_xscr", xh_scr[:, cols], xhb[:, cols],
                  waits=[xdone_ev[h * 4 + 3]])
            b.dma("g", "d_xscr", xl_scr[:, cols], xlb[:, cols],
                  waits=[xdone_ev[h * 4 + 3]])
        e_xscr = b.ev("d_xscr")

        wsred_evs = []
        for oc in range(NOC):
            gslot = []
            for tt in range(2):
                s_, bank_, rel_ = pf_rot.take()
                gslot.append((s_, bank_, rel_))
            e_mm_last = None
            for dcg in range(4):
                k = oc * 4 + dcg
                csl, e_sh, e_sl = gload[k]
                xw = [xdone_ev[dcg * 4 + i] for i in range(4)]
                banks = (gslot[0][1], gslot[1][1])
                first = dcg == 0

                def conv_mms(E, csl=csl, dcg=dcg, banks=banks, first=first):
                    ins = None
                    st = [first, first]
                    for di in range(4):
                        dc = dcg * 4 + di
                        for tap in range(3):
                            wcol = (tap * 4 + di) * 128
                            whi = cwh[csl][:, wcol:wcol + 128]
                            wlo = cwl[csl][:, wcol:wcol + 128]
                            for tt in range(2):
                                c0 = tt * 512 + tap
                                rh = xh(dc, c0, 512)
                                rl = xl(dc, c0, 512)
                                for lhsT, rhs in ((whi, rh), (whi, rl), (wlo, rh)):
                                    ins = E.matmul(banks[tt][:, 0:512], lhsT, rhs,
                                                   start=st[tt], stop=False,
                                                   skip_group_check=True)
                                    st[tt] = False
                    return ins
                waits = [e_sh, e_sl] + xw
                if dcg == 0:
                    waits += [gslot[0][2], gslot[1][2]]
                e_mm_last = b.op("t", conv_mms, waits=waits, inc="T")
                cw_rel[csl] = e_mm_last
                if k + NCW < NGRP:
                    load_group(k + NCW)
            if 4 <= oc < 12:
                for j in range(2):
                    kc = (oc - 4) * 2 + j
                    wsl, wbuf, wrel = wt_stA.take()
                    e_wl = b.dma("g", f"d_wA{wsl}", wbuf[:],
                                 fw_h[kc * 128:(kc + 1) * 128, :], waits=[wrel])
                    e_wr = b.op("v", lambda E, wbuf=wbuf, kc=kc: E.reduce_max(
                        wmax[:, kc:kc + 1], wbuf[:], axis=AX.X,
                        apply_absolute_value=True), waits=[e_wl])
                    wt_stA.release(wsl, e_wr)
                    wsred_evs.append(e_wr)
            for tt in range(2):
                s_, bank_, _ = gslot[tt]
                tsl, tmp, trel = qkv_tmp.take()
                e_e1 = b.op("v", lambda E, p=bank_, tmp=tmp, oc=oc:
                            E.tensor_scalar_add(tmp[:], p[:, 0:512],
                                                cb_s[:, oc:oc + 1]),
                            waits=[e_mm_last, trel, e_bias_dma])
                pf_rot.release(s_, e_e1)
                hsl, bh, hrel = qkv_hi.take()
                lsl, bl_, lrel = qkv_lo.take()
                e_h = b.op("v", lambda E, tmp=tmp, bh=bh: E.tensor_copy(
                    bh[:], tmp[:]), waits=[e_e1, hrel])
                e_l = b.op("g", lambda E, tmp=tmp, bh=bh, bl_=bl_:
                           E.tensor_tensor(bl_[:], tmp[:], bh[:], ALU.subtract),
                           waits=[e_h, lrel])
                qkv_tmp.release(tsl, e_l)
                eh_d = b.dma("g", f"d_qd{hsl}",
                             qkvh_scr[oc, :, tt * 512:(tt + 1) * 512], bh[:],
                             waits=[e_h])
                el_d = b.dma("g", f"d_ql{lsl}",
                             qkvl_scr[oc, :, tt * 512:(tt + 1) * 512], bl_[:],
                             waits=[e_l])
                qkv_hi.release(hsl, eh_d)
                qkv_lo.release(lsl, el_d)

        e_wm = b.op("v", lambda E: E.reduce_max(red2[:], wmax[:], axis=AX.X),
                    waits=wsred_evs, inc="V")
        sl2, bank2, rl2 = pf_rot.take()
        e_tp2 = b.op("t", lambda E, p=bank2: E.transpose(
            p[0:1, 0:128], red2[:], idn[:]),
            waits=[e_wm, rl2, e_id], inc="T")

        b.op("v", lambda E, p=bank2: E.reduce_max(
            t_row[0:1, 0:1], p[0:1, 0:128], axis=AX.X), waits=[e_tp2])
        e_ws = b.op("v", lambda E: E.tensor_scalar_max(
            ws_t[:], t_row[0:1, 0:1], 1e-6))
        pf_rot.release(sl2, e_ws)
        slw, bankw, rlw = pf_rot.take()
        e_bw = b.op("t", lambda E, p=bankw: E.matmul(
            p[:, 0:1], ones_row[0:1, :], ws_t[0:1, 0:1], start=True, stop=True,
            skip_group_check=True), waits=[e_ws, rlw, e_masks], inc="T")

        b.op("v", lambda E, p=bankw: E.tensor_copy(ws_bc[:], p[:, 0:1]),
             waits=[e_bw])
        e_rws = b.op("v", lambda E: E.reciprocal(rws[:], ws_bc[:]))
        pf_rot.release(slw, e_rws)


        conv_done = b.barrier_evs() + b.dma_evs()
        stkA.close()

        stkB = ExitStack()
        ecB = stkB.enter_context
        qin_h = [ecB(nc.sbuf_tensor(f"qinh{i}", [128, 512], BF16)) for i in range(NWC)]
        qin_l = [ecB(nc.sbuf_tensor(f"qinl{i}", [128, 512], BF16)) for i in range(NWC)]
        kin_h = [ecB(nc.sbuf_tensor(f"kinh{i}", [128, 512], BF16)) for i in range(NWC)]
        kin_l = [ecB(nc.sbuf_tensor(f"kinl{i}", [128, 512], BF16)) for i in range(NWC)]
        vin_h = [ecB(nc.sbuf_tensor(f"vinh{i}", [128, 512], BF16)) for i in range(NWC)]
        vin_l = [ecB(nc.sbuf_tensor(f"vinl{i}", [128, 512], BF16)) for i in range(NWC)]
        wsf = Rot([ecB(nc.sbuf_tensor(f"wsf{i}", [128, 512], F32)) for i in range(1)])
        wts = {}
        for nm in ("wq", "wk", "wv"):
            wts[nm] = ([ecB(nc.sbuf_tensor(f"{nm}h{i}", [128, 512], BF16))
                        for i in range(NWC)],
                       [ecB(nc.sbuf_tensor(f"{nm}l{i}", [128, 512], BF16))
                        for i in range(NWC)])
        wts["wo"] = wts["wq"]
        pwf = Rot([ecB(nc.sbuf_tensor(f"pwf{i}", [128, 1024], F32))
                   for i in range(1)])
        pw_hi = [ecB(nc.sbuf_tensor(f"pwh{i}", [128, 1024], BF16))
                 for i in range(NWC)]
        pw_lo = [ecB(nc.sbuf_tensor(f"pwl{i}", [128, 1024], BF16))
                 for i in range(NWC)]
        qh_hi = [ecB(nc.sbuf_tensor(f"qhh{i}", [128, 512], BF16)) for i in range(NWC)]
        qh_lo = [ecB(nc.sbuf_tensor(f"qhl{i}", [128, 512], BF16)) for i in range(NWC)]
        kh_hi = [ecB(nc.sbuf_tensor(f"khh{i}", [128, 512], BF16)) for i in range(NWC)]
        kh_lo = [ecB(nc.sbuf_tensor(f"khl{i}", [128, 512], BF16)) for i in range(NWC)]
        vt_hi = [ecB(nc.sbuf_tensor(f"vth{i}", [128, 512], BF16)) for i in range(NWC)]
        vt_lo = [ecB(nc.sbuf_tensor(f"vtl{i}", [128, 512], BF16)) for i in range(NWC)]
        hp_tmp = Rot([ecB(nc.sbuf_tensor(f"hpt{i}", [128, 512], F32))
                      for i in range(1)])
        mskd = Rot([ecB(nc.sbuf_tensor(f"mskd{i}", [128, 512], F32))
                    for i in range(1)])
        exr = Rot([ecB(nc.sbuf_tensor(f"exr{i}", [128, 512], F32))
                   for i in range(1)])
        pnr = Rot([ecB(nc.sbuf_tensor(f"pnr{i}", [128, 512], F32))
                   for i in range(2)])
        p_hi = [[ecB(nc.sbuf_tensor(f"phi{h}_{q}", [128, 512], BF16))
                 for q in range(4)] for h in range(2)]
        p_lo = [[ecB(nc.sbuf_tensor(f"plo{h}_{q}", [128, 512], BF16))
                 for q in range(4)] for h in range(2)]
        pTh = Rot([ecB(nc.sbuf_tensor(f"pth{i}", [128, 512], BF16))
                   for i in range(2)])
        pTl = Rot([ecB(nc.sbuf_tensor(f"ptl{i}", [128, 512], BF16))
                   for i in range(2)])
        ctx_hi = [ecB(nc.sbuf_tensor(f"cxh{i}", [128, 512], BF16))
                  for i in range(NPAIR)]
        ctx_lo = [ecB(nc.sbuf_tensor(f"cxl{i}", [128, 512], BF16))
                  for i in range(NPAIR)]
        mha_hi = [ecB(nc.sbuf_tensor(f"mhh{i}", [128, 512], BF16))
                  for i in range(NWC)]
        mha_lo = [ecB(nc.sbuf_tensor(f"mhl{i}", [128, 512], BF16))
                  for i in range(NWC)]
        xres = Rot([ecB(nc.sbuf_tensor(f"xres{i}", [128, 512], BF16))
                    for i in range(4)])

        rel_qin = [None] * NWC
        rel_kin = [None] * NWC
        rel_vin = [None] * NWC
        rel_w = {nm: [None] * NWC for nm in ("wq", "wk", "wv", "wo")}
        rel_pw = [None] * NWC
        rel_qh = [None] * NWC
        rel_kh = [None] * NWC
        rel_vt = [None] * NWC
        rel_ctx = [None] * NPAIR
        rel_mha = [None] * NWC
        rel_phil = [[None] * 4 for _ in range(2)]
        rel_x1s = [None] * NDC
        x1_out_ev = [[None] * NDC, [None] * NDC]
        x1_apply_ev = [[None] * NDC, [None] * NDC]
        nmx_i = [0]

        def split_f32(src, dst_h, dst_l, e_src, rels=(None, None), lo_eng="v"):
            e_h = b.op("v", lambda E, src=src, dst_h=dst_h: E.tensor_copy(
                dst_h[:], src[:]), waits=[e_src, rels[0]], inc="V")
            e_l = b.op(lo_eng, lambda E, src=src, dst_h=dst_h, dst_l=dst_l:
                       E.tensor_tensor(dst_l[:], src[:], dst_h[:], ALU.subtract),
                       waits=[e_h, rels[1]], inc=ENG_SEM[lo_eng])
            return e_h, e_l

        def emit_ln(src_tiles, apply_dst, src_evs, post_apply=None):
            e_stat = None
            for dc in range(NDC):
                ssl, sqt, srel = sqr.take()
                e_sq = b.op("v" if dc % 2 == 0 else "g",
                            lambda E, sqt=sqt, dc=dc, src=src_tiles:
                            E.tensor_mul(sqt[:], src[dc][:], src[dc][:]),
                            waits=[src_evs[dc], srel])

                def stat_mm(E, dc=dc, sqt=sqt, src=src_tiles):
                    E.matmul(PS0[0:1, :], ones_col[:, 0:1], src[dc][:],
                             start=(dc == 0), stop=(dc == NDC - 1),
                             skip_group_check=True)
                    return E.matmul(PS1[0:1, :], ones_col[:, 0:1], sqt[:],
                                    start=(dc == 0), stop=(dc == NDC - 1),
                                    skip_group_check=True)
                e_stat = b.op("t", stat_mm,
                              waits=[e_sq, src_evs[dc], ps_rel[0], e_masks],
                              inc="T")
                sqr.release(ssl, e_stat)
            inv_d = 1.0 / D
            b.op("v", lambda E: E.tensor_scalar_mul(m_row[:], PS0[0:1, :], inv_d),
                 waits=[e_stat])
            b.op("v", lambda E: E.tensor_scalar_mul(v_row[:], PS1[0:1, :], inv_d))
            b.op("v", lambda E: E.tensor_mul(t_row[:], m_row[:], m_row[:]))
            b.op("v", lambda E: E.tensor_sub(v_row[:], v_row[:], t_row[:]))
            e_r1 = b.op("v", lambda E: E.tensor_scalar_add(v_row[:], v_row[:],
                                                           EPS_LN))
            ps_rel[0] = e_r1
            e_sq2 = b.op("a", lambda E: E.activation(u_row[:], v_row[:], ACTF.Sqrt),
                         waits=[e_r1], inc="A")
            b.op("v", lambda E: E.reciprocal(i_row[:], u_row[:]), waits=[e_sq2])
            for _ in range(2):
                b.op("v", lambda E: E.tensor_mul(t_row[:], i_row[:], i_row[:]))
                b.op("v", lambda E: E.tensor_mul(t_row[:], t_row[:], v_row[:]))
                b.op("v", lambda E: E.tensor_scalar(t_row[:], t_row[:], -0.5, 1.5,
                                                    ALU.mult, ALU.add))
                e_r2 = b.op("v", lambda E: E.tensor_mul(i_row[:], i_row[:],
                                                        t_row[:]))
            sl_m, bank_m, rel_m = pf_rot.take()
            e_bm = b.op("t", lambda E, p=bank_m: E.matmul(
                p[:, 0:512], ones_row[0:1, :], m_row[0:1, :], start=True,
                stop=True, skip_group_check=True),
                waits=[e_r1, rel_m, e_masks], inc="T")
            sl_i, bank_i, rel_i = pf_rot.take()
            e_bi = b.op("t", lambda E, p=bank_i: E.matmul(
                p[:, 0:512], ones_row[0:1, :], i_row[0:1, :], start=True,
                stop=True, skip_group_check=True),
                waits=[e_r2, rel_i, e_masks], inc="T")
            e_cm = b.op("v", lambda E, p=bank_m: E.tensor_copy(
                m_bc_s[:], p[:, 0:512]), waits=[e_bm])
            e_ci = b.op("v", lambda E, p=bank_i: E.tensor_copy(
                i_bc_s[:], p[:, 0:512]), waits=[e_bi])
            pf_rot.release(sl_m, e_cm)
            pf_rot.release(sl_i, e_ci)
            apply_evs = []
            for dc in range(NDC):
                dst, wa = apply_dst(dc)
                eng = "v" if dc % 2 == 1 else "g"
                b.op(eng, lambda E, dc=dc, dst=dst, src=src_tiles:
                     E.tensor_tensor(dst[:], src[dc][:], m_bc_s[:],
                                     ALU.subtract),
                     waits=[e_cm, e_ci, src_evs[dc]] + list(wa))
                e_ap = b.op(eng, lambda E, dst=dst:
                            E.tensor_tensor(dst[:], dst[:], i_bc_s[:],
                                            ALU.mult))
                apply_evs.append(e_ap)
                if post_apply is not None:
                    post_apply(dc, e_ap)
            return apply_evs

        for blk in range(NB):
            c0 = blk * 512
            bw = conv_done if blk == 0 else []
            ein_q, ein_k, ein_v = [], [], []
            for wc in range(NWC):
                for (dsth, dstl, base, ein, relv) in (
                        (qin_h, qin_l, 0, ein_q, rel_qin),
                        (kin_h, kin_l, 4, ein_k, rel_kin),
                        (vin_h, vin_l, 8, ein_v, rel_vin)):
                    sname = {0: "d_qin", 4: "d_kin", 8: "d_vin"}[base]
                    e1 = b.dma("a", sname, dsth[wc][:],
                               qkvh_scr[base + wc, :, c0:c0 + 512],
                               waits=bw + [relv[wc]])
                    e2 = b.dma("a", sname, dstl[wc][:],
                               qkvl_scr[base + wc, :, c0:c0 + 512],
                               waits=[relv[wc]])
                    ein.append((e1, e2))
            qin_all = b.ev("d_qin")
            kin_all = b.ev("d_kin")
            vin_all = b.ev("d_vin")
            ein_q = [(qin_all, qin_all)] * NWC
            ein_k = [(kin_all, kin_all)] * NWC
            ein_v = [(vin_all, vin_all)] * NWC
            wsplit_ev = {}
            for name, h_ap in (("wq", wq_h), ("wk", wk_h), ("wv", wv_h)):
                dsth, dstl = wts[name]
                evs = []
                for wc in range(NWC):
                    ssl, sbuf, srel = wsf.take()
                    e_w = b.dma("a", "d_wsf", sbuf[:],
                                h_ap[wc * 128:(wc + 1) * 128, :],
                                waits=bw + [srel, rel_w[name][wc]])
                    e_h, e_l = split_f32(sbuf, dsth[wc], dstl[wc], e_w,
                                         rels=(rel_w[name][wc],
                                               rel_w[name][wc]),
                                         lo_eng="v" if wc % 2 == 0 else "g")
                    wsf.release(ssl, e_l)
                    evs.append((e_h, e_l))
                wsplit_ev[name] = evs

            def hp_group(dst_hi, dst_lo, in_h, in_l, ein, wname, bias_kind,
                         rel_dst, vt_mode=False):
                whi, wlo = wts[wname]
                out_evs = []
                for i in range(NWC):
                    psl, pbank, prel = pf_rot.take()

                    def mms(E, i=i, pbank=pbank, in_h=tuple(in_h),
                            in_l=tuple(in_l), whi=tuple(whi), wlo=tuple(wlo),
                            vt_mode=vt_mode):
                        ins = None
                        st = True
                        for wc in range(NWC):
                            if vt_mode:
                                l_h = in_h[wc][:, i * 128:(i + 1) * 128]
                                l_l = in_l[wc][:, i * 128:(i + 1) * 128]
                                r_h, r_l = whi[wc][:], wlo[wc][:]
                            else:
                                l_h = whi[wc][:, i * 128:(i + 1) * 128]
                                l_l = wlo[wc][:, i * 128:(i + 1) * 128]
                                r_h, r_l = in_h[wc][:], in_l[wc][:]
                            for lhsT, rhs in ((l_h, r_h), (l_l, r_h),
                                              (l_h, r_l)):
                                ins = E.matmul(pbank[:, 0:512], lhsT, rhs,
                                               start=st, stop=False,
                                               skip_group_check=True)
                                st = False
                        return ins
                    waits = [prel]
                    for wc in range(NWC):
                        waits += [ein[wc][0], ein[wc][1],
                                  wsplit_ev[wname][wc][0],
                                  wsplit_ev[wname][wc][1]]
                    e_mm = b.op("t", mms, waits=waits, inc="T")
                    tsl, tmp, trel = hp_tmp.take()
                    if bias_kind == "q":
                        e_e = b.op("v", lambda E, p=pbank, tmp=tmp, i=i:
                                   E.tensor_scalar(tmp[:], p[:, 0:512], 0.125,
                                                   bqs_s[:, i:i + 1],
                                                   ALU.mult, ALU.add),
                                   waits=[e_mm, trel, e_bqs], inc="V")
                    elif bias_kind == "k":
                        e_e = b.op("a", lambda E, p=pbank, tmp=tmp, i=i: E.add(
                            tmp[:], p[:, 0:512], bk_s[:, i:i + 1]),
                            waits=[e_mm, trel, e_bias_dma], inc="A")
                    else:
                        e_e = b.op("v", lambda E, p=pbank, tmp=tmp:
                                   E.tensor_tensor(tmp[:], p[:, 0:512],
                                                   bv_bc[:], ALU.add),
                                   waits=[e_mm, trel, e_bvbc], inc="V")
                    pf_rot.release(psl, e_e)
                    e_h, e_l = split_f32(tmp, dst_hi[i], dst_lo[i], e_e,
                                         rels=(rel_dst[i], rel_dst[i]),
                                         lo_eng="v" if i % 2 == 0 else "g")
                    hp_tmp.release(tsl, e_l)
                    out_evs.append((e_h, e_l))
                return out_evs
            eqh = hp_group(qh_hi, qh_lo, qin_h, qin_l, ein_q, "wq", "q", rel_qh)
            ekh = hp_group(kh_hi, kh_lo, kin_h, kin_l, ein_k, "wk", "k", rel_kh)
            evt = hp_group(vt_hi, vt_lo, vin_h, vin_l, ein_v, "wv", "v", rel_vt,
                           vt_mode=True)
            for wc in range(NWC):
                rel_qin[wc] = eqh[-1][1]
                rel_kin[wc] = ekh[-1][1]
                rel_vin[wc] = evt[-1][1]
            qh_ready = [e for pr in eqh for e in pr]
            kh_ready = [e for pr in ekh for e in pr]
            vt_ready = [e for pr in evt for e in pr]

            dsth, dstl = wts["wo"]
            evs = []
            for wc in range(NWC):
                ssl, sbuf, srel = wsf.take()
                e_w = b.dma("a", "d_wsf", sbuf[:],
                            wo_h[wc * 128:(wc + 1) * 128, :],
                            waits=[srel, rel_w["wo"][wc]] + qh_ready)
                e_h, e_l = split_f32(sbuf, dsth[wc], dstl[wc], e_w,
                                     rels=(rel_w["wo"][wc],
                                           rel_w["wo"][wc]),
                                     lo_eng="v" if wc % 2 == 0 else "g")
                wsf.release(ssl, e_l)
                evs.append((e_h, e_l))
            wsplit_ev["wo"] = evs

            def emit_pw_stream(half):
                evs = []
                for wc in range(NWC):
                    fsl, fbuf, frel = pwf.take()
                    e_w = b.dma("s", ("d_pwf0", "d_pwf1")[fsl], fbuf[:],
                                pw_h[wc * 128:(wc + 1) * 128,
                                     half * 1024:(half + 1) * 1024],
                                waits=[frel] + (bw if blk == 0 else []))
                    e_h = b.op("a", lambda E, fbuf=fbuf, wc=wc: E.copy(
                        pw_hi[wc][:], fbuf[:]),
                        waits=[e_w, rel_pw[wc]], inc="A")
                    e_l = b.op("v", lambda E, fbuf=fbuf, wc=wc: E.tensor_tensor(
                        pw_lo[wc][:], fbuf[:], pw_hi[wc][:], ALU.subtract),
                        waits=[e_h], inc="V")
                    pwf.release(fsl, e_l)
                    evs += [e_h, e_l]
                return evs
            pw_stream_evs = {}
            pw_stream_evs[0] = emit_pw_stream(0)

            for p in range(NPAIR):
                for qt in range(4):
                    for hh in range(2):
                        psl, pbank, prel = pf_rot.take()
                        r0 = hh * 64

                        def smm(E, p_=p, qt=qt, r0=r0, pbank=pbank):
                            ins = None
                            st = True
                            qsl = slice(qt * 128, (qt + 1) * 128)
                            for lhsT, rhs in (
                                    (qh_hi[p_][r0:r0 + 64, qsl],
                                     kh_hi[p_][r0:r0 + 64, :]),
                                    (qh_lo[p_][r0:r0 + 64, qsl],
                                     kh_hi[p_][r0:r0 + 64, :]),
                                    (qh_hi[p_][r0:r0 + 64, qsl],
                                     kh_lo[p_][r0:r0 + 64, :])):
                                ins = E.matmul(pbank[:, 0:512], lhsT, rhs,
                                               start=st, stop=False,
                                               skip_group_check=True,
                                               tile_position=(r0, 0))
                                st = False
                            return ins
                        e_mm = b.op("t", smm,
                                    waits=[prel] + qh_ready + kh_ready, inc="T")
                        msl, mbuf, mrel = mskd.take()
                        ni = nmx_i[0] % 16
                        nmx_i[0] += 1
                        nmx = nmx_a[:, ni:ni + 1]
                        sums = sum_a[:, ni:ni + 1]
                        rec = rec_a[:, ni:ni + 1]

                        b.op("v", lambda E, pbank=pbank, mbuf=mbuf, qt=qt:
                             E.tensor_tensor(mbuf[:], pbank[:, 0:512],
                                             amask[qt][:], ALU.add),
                             waits=[e_mm, mrel, e_masks])
                        e_msk = b.op("v", lambda E, mbuf=mbuf, nmx=nmx:
                                     E.reduce_max(nmx, mbuf[:], axis=AX.X,
                                                  negate=True))
                        pf_rot.release(psl, e_msk)
                        esl, ebuf, erel = exr.take()
                        e_exp = b.op("a", lambda E, mbuf=mbuf, ebuf=ebuf,
                                     nmx=nmx, sums=sums: E.activation(
                                         ebuf[:], mbuf[:], ACTF.Exp, bias=nmx,
                                         scale=1.0, accum_out=sums),
                                     waits=[e_msk, erel], inc="A")
                        mskd.release(msl, e_exp)
                        e_rc = b.op("v", lambda E, rec=rec, sums=sums:
                                    E.reciprocal(rec, sums),
                                    waits=[e_exp])
                        nsl, pnb, nrel = pnr.take()
                        e_pm = b.op("a", lambda E, ebuf=ebuf, rec=rec, pnb=pnb:
                                    E.mul(pnb[:], ebuf[:], rec),
                                    waits=[e_rc, nrel], inc="A")
                        exr.release(esl, e_pm)
                        e_hi = b.op("v", lambda E, pnb=pnb, hh=hh, qt=qt:
                                    E.tensor_copy(p_hi[hh][qt][:], pnb[:]),
                                    waits=[e_pm, rel_phil[hh][qt]])
                        e_lo = b.op("g", lambda E, pnb=pnb, hh=hh, qt=qt:
                                    E.tensor_tensor(p_lo[hh][qt][:], pnb[:],
                                                    p_hi[hh][qt][:],
                                                    ALU.subtract),
                                    waits=[e_hi])
                        pnr.release(nsl, e_lo)
                        rel_phil[hh][qt] = e_lo
                csl, cbank, crel = pf_rot.take()
                e_ctx_last = None
                for hh in range(2):
                    philo = [rel_phil[hh][q] for q in range(4)]
                    e_tp_last = None
                    for sc in range(4):
                        def tp_fn(E, hh=hh, sc=sc):
                            ins = None
                            for qt in range(4):
                                E.transpose(
                                    PT0[:, qt * 128:(qt + 1) * 128],
                                    p_hi[hh][qt][:, sc * 128:(sc + 1) * 128],
                                    idnb[:])
                                ins = E.transpose(
                                    PT1[:, qt * 128:(qt + 1) * 128],
                                    p_lo[hh][qt][:, sc * 128:(sc + 1) * 128],
                                    idnb[:])
                            return ins
                        e_tp = b.op("t", tp_fn,
                                    waits=philo + [pt_rel[0], pt_rel[1],
                                                   e_idnb], inc="T")
                        e_tp_last = e_tp
                        hsl_, hbuf_, hrel_ = pTh.take()
                        lsl_, lbuf_, lrel_ = pTl.take()
                        e_ph = b.op("a", lambda E, hbuf_=hbuf_: E.copy(
                            hbuf_[:], PT0[:, 0:512]),
                            waits=[e_tp, hrel_], inc="A")
                        e_pl = b.op("v", lambda E, lbuf_=lbuf_: E.tensor_copy(
                            lbuf_[:], PT1[:, 0:512]),
                            waits=[e_tp, lrel_])
                        pt_rel[0] = e_pl
                        pt_rel[1] = e_pl
                        vcol = p * 128 + hh * 64

                        def ctx_fn(E, hh=hh, sc=sc, vcol=vcol, cbank=cbank,
                                   hbuf_=hbuf_, lbuf_=lbuf_):
                            ins = None
                            st = (sc == 0)
                            for lhsT, rhs in (
                                    (vt_hi[sc][:, vcol:vcol + 64], hbuf_[:]),
                                    (vt_hi[sc][:, vcol:vcol + 64], lbuf_[:]),
                                    (vt_lo[sc][:, vcol:vcol + 64], hbuf_[:])):
                                ins = E.matmul(
                                    cbank[hh * 64:(hh + 1) * 64, 0:512],
                                    lhsT, rhs, start=st, stop=False,
                                    skip_group_check=True,
                                    tile_position=(0, hh * 64))
                                st = False
                            return ins
                        waits = [e_ph, e_pl] + vt_ready
                        if sc == 0 and hh == 0:
                            waits.append(crel)
                        e_ctx_last = b.op("t", ctx_fn, waits=waits, inc="T")
                        pTh.release(hsl_, e_ctx_last)
                        pTl.release(lsl_, e_ctx_last)
                    for qt in range(4):
                        rel_phil[hh][qt] = e_tp_last
                e_ch = b.op("a", lambda E, p=p, cbank=cbank: E.copy(
                    ctx_hi[p][:], cbank[:, 0:512]),
                    waits=[e_ctx_last, rel_ctx[p]], inc="A")
                e_cl = b.op("v", lambda E, p=p, cbank=cbank: E.tensor_tensor(
                    ctx_lo[p][:], cbank[:, 0:512], ctx_hi[p][:], ALU.subtract),
                    waits=[e_ch], inc="V")
                pf_rot.release(csl, e_cl)
                rel_ctx[p] = e_cl

            ctx_ready = [rel_ctx[p] for p in range(NPAIR)]

            mha_evs = []
            woh, wol = wts["wo"]
            for oc in range(NWC):
                psl, pbank, prel = pf_rot.take()

                def womm(E, oc=oc, pbank=pbank, woh=tuple(woh), wol=tuple(wol)):
                    ins = None
                    st = True
                    for p in range(NPAIR):
                        osl = slice(oc * 128, (oc + 1) * 128)
                        for lhsT, rhs in ((woh[p][:, osl], ctx_hi[p][:]),
                                          (wol[p][:, osl], ctx_hi[p][:]),
                                          (woh[p][:, osl], ctx_lo[p][:])):
                            ins = E.matmul(pbank[:, 0:512], lhsT, rhs,
                                           start=st, stop=False,
                                           skip_group_check=True)
                            st = False
                    return ins
                waits = [prel] + ctx_ready
                for p in range(NPAIR):
                    waits += [wsplit_ev["wo"][p][0], wsplit_ev["wo"][p][1]]
                e_mm = b.op("t", womm, waits=waits, inc="T")
                tsl, tmp, trel = hp_tmp.take()
                e_e = b.op("a", lambda E, pbank=pbank, tmp=tmp, oc=oc: E.add(
                    tmp[:], pbank[:, 0:512], bo_s[:, oc:oc + 1]),
                    waits=[e_mm, trel], inc="A")
                pf_rot.release(psl, e_e)
                e_h, e_l = split_f32(tmp, mha_hi[oc], mha_lo[oc], e_e,
                                     rels=(rel_mha[oc], rel_mha[oc]),
                                     lo_eng="v" if oc % 2 == 0 else "g")
                hp_tmp.release(tsl, e_l)
                mha_evs += [e_h, e_l]
            for p in range(NPAIR):
                rel_ctx[p] = mha_evs[-1]

            x1pre_evs = [None] * NDC
            for half in range(2):
                pw_evs_half = pw_stream_evs[half]
                e_mm_half = None
                for dc in range(half * 8, half * 8 + 8):
                    rsl_h, rbh, rrel_h = xres.take()
                    rsl_l, rbl, rrel_l = xres.take()
                    xo = dc * (TOK + 2) + 2 + c0
                    e_rh = b.dma("s", f"d_xr{rsl_h}", rbh[:],
                                 xh_scr[:, xo:xo + 512],
                                 waits=[rrel_h, e_xscr])
                    e_rl = b.dma("s", f"d_xr{rsl_l}", rbl[:],
                                 xl_scr[:, xo:xo + 512],
                                 waits=[rrel_l])
                    psl, pbank, prel = pf_rot.take()
                    lcol = dc * 128 - half * 1024

                    def pjmm(E, lcol=lcol, pbank=pbank, rbh=rbh, rbl=rbl):
                        ins = None
                        st = True
                        dsl = slice(lcol, lcol + 128)
                        for wc in range(NWC):
                            for lhsT, rhs in (
                                    (pw_hi[wc][:, dsl], mha_hi[wc][:]),
                                    (pw_lo[wc][:, dsl], mha_hi[wc][:]),
                                    (pw_hi[wc][:, dsl], mha_lo[wc][:])):
                                ins = E.matmul(pbank[:, 0:512], lhsT, rhs,
                                               start=st, stop=False,
                                               skip_group_check=True)
                                st = False
                        E.matmul(pbank[:, 0:512], idnb[:], rbh[:], start=False,
                                 stop=False, skip_group_check=True)
                        return E.matmul(pbank[:, 0:512], idnb[:], rbl[:],
                                        start=False, stop=True,
                                        skip_group_check=True)
                    waits = [prel, e_rh, e_rl, e_idnb] + mha_evs + pw_evs_half
                    e_mm = b.op("t", pjmm, waits=waits, inc="T")
                    e_mm_half = e_mm
                    xres.release(rsl_h, e_mm)
                    xres.release(rsl_l, e_mm)
                    e_e = b.op("a", lambda E, pbank=pbank, dc=dc: E.add(
                        x1s[dc][:], pbank[:, 0:512], pb_s[:, dc:dc + 1]),
                        waits=[e_mm, rel_x1s[dc]], inc="A")
                    pf_rot.release(psl, e_e)
                    x1pre_evs[dc] = e_e
                for wc in range(NWC):
                    rel_pw[wc] = e_mm_half
                if half == 0:
                    pw_stream_evs[1] = emit_pw_stream(1)
            for wc in range(NWC):
                rel_mha[wc] = x1pre_evs[-1]
                rel_qh[wc] = x1pre_evs[-1]
                rel_kh[wc] = x1pre_evs[-1]
                rel_vt[wc] = x1pre_evs[-1]
                for nm in ("wq", "wk", "wv", "wo"):
                    rel_w[nm][wc] = x1pre_evs[-1]

            apply_evs = emit_ln(x1s, lambda dc: (x1s[dc], []), x1pre_evs)
            for dc in range(NDC):
                if blk == 0:
                    e_mx = b.op("v", lambda E, dc=dc: E.reduce_max(
                        chmax[:, dc:dc + 1], x1s[dc][:], axis=AX.X,
                        apply_absolute_value=True),
                        waits=[apply_evs[dc]], inc="V")
                else:
                    b.op("v", lambda E, dc=dc: E.reduce_max(
                        red1[:], x1s[dc][:], axis=AX.X,
                        apply_absolute_value=True), waits=[apply_evs[dc]])
                    e_mx = b.op("v", lambda E, dc=dc: E.tensor_max(
                        chmax[:, dc:dc + 1], chmax[:, dc:dc + 1], red1[:]))
                x1_apply_ev[blk][dc] = e_mx
                e_d = b.dma("g", "d_x1o", x1_scr[dc, :, c0:c0 + 512],
                            x1s[dc][:], waits=[apply_evs[dc]])
                x1_out_ev[blk][dc] = e_d
            x1o_all = b.ev("d_x1o")
            for dc in range(NDC):
                x1_out_ev[blk][dc] = x1o_all
                if blk == 0:
                    rel_x1s[dc] = x1o_all

        attn_done = b.barrier_evs() + b.dma_evs()
        stkB.close()

        stkC = ExitStack()
        ecC = stkC.enter_context
        wt_st = Rot([ecC(nc.sbuf_tensor(f"wtst{i}", [128, 2048], BF16))
                     for i in range(2)])
        wq8 = [ecC(nc.sbuf_tensor(f"wq8_{i}", [128, 2048], F8))
               for i in range(NDC)]
        xq8_2 = [[ecC(nc.sbuf_tensor(f"xq8_{s}_{i}", [128, 512], F8))
                  for i in range(NDC)] for s in range(2)]
        yr = Rot([ecC(nc.sbuf_tensor(f"y{i}", [128, 512], BF16))
                  for i in range(6)])
        xstg = Rot([ecC(nc.sbuf_tensor(f"xstg{i}", [128, 512], F32))
                    for i in range(2)])
        outr = Rot([ecC(nc.sbuf_tensor(f"or{i}", [128, 512], F32))
                    for i in range(4)])

        e_lm = b.op("v", lambda E: E.reduce_max(red1[:], chmax[:], axis=AX.X),
                    waits=[x1_apply_ev[1][dc] for dc in range(NDC)], inc="V")
        sl1, bank1, rl1 = pf_rot.take()
        e_tp1 = b.op("t", lambda E, p=bank1: E.transpose(
            p[0:1, 0:128], red1[:], idn[:]),
            waits=[e_lm, rl1, e_id], inc="T")
        e_xm = b.op("v", lambda E, p=bank1: E.reduce_max(
            cc_s[0:1, 0:1], p[0:1, 0:128], axis=AX.X),
            waits=[e_tp1, e_masks], inc="V")
        pf_rot.release(sl1, e_xm)
        e_ccw = b.dma("g", "d_ccd", cc_in.ap()[:, :], cc_s[:], waits=[e_xm])
        e_cc = b.op("g", lambda E: E.collective_compute(
            "AllReduce", ALU.max, replica_groups=[list(range(NCORE))],
            ins=[cc_in.ap().opt()], outs=[cc_out.ap().opt()]),
            waits=[e_ccw], inc="C")
        e_ccr = b.dma("g", "d_ccd", cc_s[:], cc_out.ap()[:, :], waits=[e_cc])
        wq8_evs = []
        for kc in range(NDC):
            ssl, sbuf, srel = wt_st.take()
            e_w = b.dma("s", f"d_wt{ssl}", sbuf[:],
                        fw_h[:, kc * 128:(kc + 1) * 128],
                        waits=attn_done + [srel], transpose=True)
            e_q = b.op("v", lambda E, sbuf=sbuf, kc=kc: E.tensor_scalar(
                wq8[kc][:], sbuf[:], rws[:, 0:1], None, ALU.mult),
                waits=[e_w, e_rws])
            wt_st.release(ssl, e_q)
            wq8_evs.append(e_q)

        e_xs = b.op("v", lambda E: E.tensor_scalar_max(
            xs_t[:], cc_s[0:1, 0:1], 1e-6), waits=[e_ccr], inc="V")
        slx, bankx, rlx = pf_rot.take()
        e_bx = b.op("t", lambda E, p=bankx: E.matmul(
            p[:, 0:1], ones_row[0:1, :], xs_t[0:1, 0:1], start=True, stop=True,
            skip_group_check=True), waits=[e_xs, rlx, e_masks], inc="T")

        b.op("v", lambda E, p=bankx: E.tensor_copy(xs_bc[:], p[:, 0:1]),
             waits=[e_bx])
        e_rxs = b.op("v", lambda E: E.reciprocal(rxs[:], xs_bc[:]))
        pf_rot.release(slx, e_rxs)

        e_sy = b.op("v", lambda E: E.tensor_mul(sy_bc[:], xs_bc[:], ws_bc[:]),
                    waits=[e_rws, e_rxs])

        xq0_evs = []
        for dc in range(NDC):
            xsl, xbuf2, xrel2 = xstg.take()
            e_r = b.dma("s", f"d_xs{xsl}", xbuf2[:],
                        x1_scr[dc, :, 0:512], waits=[xrel2])
            e_q = b.op("v", lambda E, dc=dc, xbuf2=xbuf2:
                       E.tensor_scalar(xq8_2[0][dc][:], xbuf2[:],
                                       rxs[:, 0:1], None, ALU.mult),
                       waits=[e_r, e_rxs], inc="V")
            xstg.release(xsl, e_q)
            xq0_evs.append(e_q)

        ffn_mm_done = [None]
        for tt in (1, 0):
            c0 = tt * 512
            xq8 = xq8_2[tt]
            xq_evs = []
            if tt == 1:
                x1_evs = [x1_out_ev[1][dc] for dc in range(NDC)]
                for dc in range(NDC):
                    e_q = b.op("v", lambda E, dc=dc, xq8=xq8: E.tensor_scalar(
                        xq8[dc][:], x1s[dc][:], rxs[:, 0:1], None, ALU.mult),
                        waits=[x1_evs[dc], e_rxs], inc="V")
                    xq_evs.append(e_q)
            else:
                xq_evs = xq0_evs
                for dc in range(NDC):
                    b.dma("s", "d_x1r", x1s[dc][:], x1_scr[dc, :, 0:512],
                          waits=[rel_x1s[dc]])
                x1_evs = [b.ev("d_x1r")] * NDC
            x2_evs = [None] * NDC
            for oc_ in range(NDC):
                psl, pbank, prel = pf_rot.take()

                def fmm(E, oc_=oc_, pbank=pbank, xq8=xq8):
                    ins = None
                    for kc in range(NDC):
                        ins = E.matmul(pbank[:, 0:512],
                                       wq8[kc][:, oc_ * 128:(oc_ + 1) * 128],
                                       xq8[kc][:], start=(kc == 0),
                                       stop=(kc == NDC - 1),
                                       skip_group_check=True)
                    return ins
                e_mm = b.op("t", fmm, waits=[prel] + xq_evs + wq8_evs, inc="T")
                ysl, ybuf, yrel = yr.take()
                y2sl, y2buf, y2rel = yr.take()
                e_y = b.op("v", lambda E, pbank=pbank, ybuf=ybuf:
                           E.tensor_scalar(ybuf[:], pbank[:, 0:512],
                                           sy_bc[:, 0:1], None, ALU.mult),
                           waits=[e_mm, yrel, e_sy], inc="V")
                pf_rot.release(psl, e_y)
                e_la = b.op("g", lambda E, ybuf=ybuf, y2buf=y2buf:
                            E.tensor_copy(y2buf[:].bitcast(U16),
                                          ybuf[:].bitcast(U16)),
                            waits=[e_y, y2rel])
                yr.release(ysl, e_la)

                e_yb = b.op("v", lambda E, y2buf=y2buf, oc_=oc_: E.tensor_scalar(
                    y2buf[:], y2buf[:], fb_s[:, oc_:oc_ + 1], None, ALU.add),
                    waits=[e_la, e_fb, xq_evs[oc_]])
                e_x2 = b.op("g", lambda E, y2buf=y2buf, oc_=oc_:
                            E.tensor_tensor(x1s[oc_][:], y2buf[:],
                                            x1s[oc_][:], ALU.add),
                            waits=[e_yb, x1_evs[oc_]])
                yr.release(y2sl, e_x2)
                x2_evs[oc_] = e_x2
            ffn_mm_done[0] = b.ev("T")
            out_slots = {}

            def apply_dst(dc):
                osl_, obuf, orel = outr.take()
                out_slots[dc] = (obuf, osl_)
                return obuf, [orel]

            def post_apply(dc, e_ap, c0=c0, tt=tt):
                obuf, osl_ = out_slots[dc]
                e_o = b.dma("g", f"d_ot{osl_}",
                            out_h[dc * 128:(dc + 1) * 128, c0:c0 + 512],
                            obuf[:], waits=[e_ap])
                outr.release(osl_, e_o)
                if tt == 1:
                    rel_x1s[dc] = e_ap
            ln2_evs = emit_ln(x1s, apply_dst, x2_evs, post_apply=post_apply)

        b.op("s", lambda E: E.nop(), waits=b.dma_evs())
        b.op("g", lambda E: E.nop(), waits=b.dma_evs())
        stkC.close()

        block = top.enter_context(nc.Block())

        @block.tensor
        def _(e):
            for f in b.q["t"]:
                f(e)

        @block.vector
        def _(e):
            for f in b.q["v"]:
                f(e)

        @block.scalar
        def _(e):
            for f in b.q["a"]:
                f(e)

        @block.gpsimd
        def _(e):
            for f in b.q["g"]:
                f(e)

        @block.sync
        def _(e):
            for f in b.q["s"]:
                f(e)

    return nc


_CACHE = {}


def kernel(**inputs):
    x = np.asarray(inputs["L0_attn_x"])[0]
    cw = np.ascontiguousarray(inputs["L0_attn_conv_w"])
    cb = np.ascontiguousarray(inputs["L0_attn_conv_b"])
    wq = np.ascontiguousarray(np.asarray(inputs["L0_attn_wq"]).reshape(W, W))
    wk = np.ascontiguousarray(np.asarray(inputs["L0_attn_wk"]).reshape(W, W))
    wv = np.ascontiguousarray(np.asarray(inputs["L0_attn_wv"]).reshape(W, W))
    bq = np.ascontiguousarray(np.asarray(inputs["L0_attn_bq"]).reshape(W))
    bk = np.ascontiguousarray(np.asarray(inputs["L0_attn_bk"]).reshape(W))
    bv = np.ascontiguousarray(np.asarray(inputs["L0_attn_bv"]).reshape(W))
    wo = np.ascontiguousarray(np.asarray(inputs["L0_attn_wo"]).reshape(W, W))
    bo = np.ascontiguousarray(inputs["L0_attn_bo"])
    pw = np.ascontiguousarray(inputs["L0_attn_proj_w"])
    pb = np.ascontiguousarray(inputs["L0_attn_proj_b"])
    fw = np.ascontiguousarray(inputs["L0_ffn_weight"])
    fb = np.ascontiguousarray(inputs["L0_ffn_bias"])

    if "nc" not in _CACHE:
        _CACHE["nc"] = build_program()
    nc = _CACHE["nc"]

    in_maps = []
    for c in range(NCORE):
        xs = np.zeros((TOK + 2, D), dtype=np.float32)
        lo = c * TOK
        xs[2:] = x[lo:lo + TOK]
        if c > 0:
            xs[0:2] = x[lo - 2:lo]
        in_maps.append({
            "x": xs, "conv_w": cw, "conv_b": cb,
            "wq": wq, "wk": wk, "wv": wv, "bq": bq, "bk": bk, "bv": bv,
            "wo": wo, "bo": bo, "proj_w": pw, "proj_b": pb,
            "ffn_w": fw, "ffn_b": fb,
        })
    res = run_bass_kernel_spmd(nc, in_maps, core_ids=list(range(NCORE)))
    full = np.concatenate([res.results[c]["out"] for c in range(NCORE)], axis=1)
    return np.ascontiguousarray(full.T)[None].astype(np.float32)


# version 18
# speedup vs baseline: 1.0025x; 1.0025x over previous
import numpy as np
from contextlib import ExitStack

import concourse.bass as bass
import concourse.mybir as mybir
from concourse.bass_utils import run_bass_kernel_spmd

F32 = mybir.dt.float32
BF16 = mybir.dt.bfloat16
F8 = mybir.dt.float8e4
U16 = mybir.dt.uint16
AX = mybir.AxisListType
ALU = mybir.AluOpType
ACTF = mybir.ActivationFunctionType

NCORE = 8
TOK = 1024
NB = 2
D = 2048
NDC = D // 128
QKV = 1536
NOC = QKV // 128
W = 512
NWC = W // 128
NPAIR = 4
EPS_LN = 1e-5
ENG_SEM = {"v": "V", "g": "G", "a": "A", "t": "T"}


class Bld:
    def __init__(self, nc):
        self.nc = nc
        self.q = {e: [] for e in "tvags"}
        self.sems = {}
        self.cnt = {}
        self.hwm = {e: {} for e in "tvags"}

    def mksem(self, stk, name):
        self.sems[name] = stk.enter_context(self.nc.semaphore(name))
        self.cnt[name] = 0

    def ev(self, name):
        return (name, self.cnt[name])

    def op(self, eng, fn, waits=(), inc=None, amt=1):
        if eng in "vag":
            waits = list(waits)
            own = ENG_SEM[eng]
            if self.cnt[own] > 0:
                waits.append((own, self.cnt[own]))
            if inc is None:
                inc, amt = own, 1
        wl = []
        for wv in waits:
            if wv is None:
                continue
            s, v = wv
            if v <= 0:
                continue
            h = self.hwm[eng]
            if h.get(s, 0) >= v:
                continue
            h[s] = v
            wl.append((s, v))
        if inc is not None:
            self.cnt[inc] += amt
        sems = self.sems

        def run(E, wl=wl, fn=fn, inc=inc, amt=amt):
            for s, v in wl:
                E.wait_ge(sems[s], v)
            ins = fn(E)
            if inc is not None:
                ins.then_inc(sems[inc], amt)
        self.q[eng].append(run)
        return (inc, self.cnt[inc]) if inc is not None else None

    def dma(self, qeng, sem, out, in_, waits=(), **kw):
        return self.op(qeng, lambda E: E.dma_start(out=out, in_=in_, **kw),
                       waits, inc=sem, amt=16)

    def dma_evs(self):
        return [self.ev(s) for s in self.cnt if s.startswith("d_")]

    def barrier_evs(self):
        return [self.ev(s) for s in ("T", "V", "A", "G")]


class Rot:
    def __init__(self, items):
        self.items = list(items)
        self.rel = [None] * len(self.items)
        self.i = 0

    def take(self):
        s = self.i % len(self.items)
        self.i += 1
        return s, self.items[s], self.rel[s]

    def release(self, s, ev):
        self.rel[s] = ev


def build_program():
    nc = bass.Bass("TRN2", target_bir_lowering=False)

    x_h = nc.dram_tensor("x", [TOK + 2, D], F32, kind="ExternalInput").ap()
    cw_h = nc.dram_tensor("conv_w", [3, D, QKV], F32, kind="ExternalInput").ap()
    cb_h = nc.dram_tensor("conv_b", [QKV], F32, kind="ExternalInput").ap()
    wq_h = nc.dram_tensor("wq", [W, W], F32, kind="ExternalInput").ap()
    wk_h = nc.dram_tensor("wk", [W, W], F32, kind="ExternalInput").ap()
    wv_h = nc.dram_tensor("wv", [W, W], F32, kind="ExternalInput").ap()
    bq_h = nc.dram_tensor("bq", [W], F32, kind="ExternalInput").ap()
    bk_h = nc.dram_tensor("bk", [W], F32, kind="ExternalInput").ap()
    bv_h = nc.dram_tensor("bv", [W], F32, kind="ExternalInput").ap()
    wo_h = nc.dram_tensor("wo", [W, W], F32, kind="ExternalInput").ap()
    bo_h = nc.dram_tensor("bo", [W], F32, kind="ExternalInput").ap()
    pw_h = nc.dram_tensor("proj_w", [W, D], F32, kind="ExternalInput").ap()
    pb_h = nc.dram_tensor("proj_b", [D], F32, kind="ExternalInput").ap()
    fw_h = nc.dram_tensor("ffn_w", [D, D], BF16, kind="ExternalInput").ap()
    fb_h = nc.dram_tensor("ffn_b", [D], BF16, kind="ExternalInput").ap()
    out_h = nc.dram_tensor("out", [D, TOK], F32, kind="ExternalOutput").ap()

    xh_scr = nc.dram_tensor("xh_scr", [128, NDC * (TOK + 2)], BF16).ap()
    xl_scr = nc.dram_tensor("xl_scr", [128, NDC * (TOK + 2)], BF16).ap()
    qkvh_scr = nc.dram_tensor("qkvh_scr", [NOC, 128, TOK], BF16).ap()
    qkvl_scr = nc.dram_tensor("qkvl_scr", [NOC, 128, TOK], BF16).ap()
    x1_scr = nc.dram_tensor("x1_scr", [NDC, 128, TOK], F32).ap()
    cc_in = nc.dram_tensor("cc_in", [1, 8], F32)
    cc_out = nc.dram_tensor("cc_out", [1, 8], F32)

    b = Bld(nc)

    with ExitStack() as top:
        for s in (["T", "V", "A", "G", "C"]
                  + ["d_bias", "d_xt0", "d_xt1", "d_cw0", "d_cw1", "d_cw2",
                     "d_qd0", "d_qd1", "d_qd2", "d_qd3",
                     "d_ql0", "d_ql1", "d_ql2", "d_ql3",
                     "d_qin", "d_kin", "d_vin", "d_wsf", "d_pwf0", "d_pwf1",
                     "d_xr0", "d_xr1", "d_xr2", "d_xr3",
                     "d_x1o", "d_x1r", "d_wt0", "d_wt1",
                     "d_ot0", "d_ot1", "d_ot2", "d_ot3",
                     "d_ccd", "d_xscr", "d_fb", "d_wA0", "d_wA1", "d_xs0", "d_xs1"]):
            b.mksem(top, s)
        ec = top.enter_context

        PF = [ec(nc.psum_tensor(f"pf{i}", [128, 512], F32)) for i in range(4)]
        PT0 = ec(nc.psum_tensor("pt0", [128, 1024], BF16))
        PT1 = ec(nc.psum_tensor("pt1", [128, 1024], BF16))
        PS0 = ec(nc.psum_tensor("ps0", [1, 512], F32))
        PS1 = ec(nc.psum_tensor("ps1", [1, 512], F32))
        pf_rot = Rot(PF)
        pt_rel = [None, None]
        ps_rel = [None]

        idn = ec(nc.sbuf_tensor("idn", [128, 128], F32))
        idnb = ec(nc.sbuf_tensor("idnb", [128, 128], BF16))
        ones_col = ec(nc.sbuf_tensor("ones_col", [128, 1], F32))
        ones_row = ec(nc.sbuf_tensor("ones_row", [1, 128], F32))
        amask = [ec(nc.sbuf_tensor(f"amask{i}", [128, 512], BF16)) for i in range(4)]
        cb_s = ec(nc.sbuf_tensor("cb_s", [128, NOC], F32))
        bqs_s = ec(nc.sbuf_tensor("bqs_s", [128, NWC], F32))
        bq_s = ec(nc.sbuf_tensor("bq_s", [128, NWC], F32))
        bk_s = ec(nc.sbuf_tensor("bk_s", [128, NWC], F32))
        bo_s = ec(nc.sbuf_tensor("bo_s", [128, NWC], F32))
        pb_s = ec(nc.sbuf_tensor("pb_s", [128, NDC], F32))
        fb_s = ec(nc.sbuf_tensor("fb_s", [128, NDC], F32))
        bv_row = ec(nc.sbuf_tensor("bv_row", [1, 512], F32))
        bv_bc = ec(nc.sbuf_tensor("bv_bc", [128, 512], F32))
        x1s = [ec(nc.sbuf_tensor(f"x1s{i}", [128, 512], F32)) for i in range(NDC)]
        sqr = Rot([ec(nc.sbuf_tensor(f"sq{i}", [128, 512], F32)) for i in range(2)])
        m_row = ec(nc.sbuf_tensor("m_row", [1, 512], F32))
        v_row = ec(nc.sbuf_tensor("v_row", [1, 512], F32))
        i_row = ec(nc.sbuf_tensor("i_row", [1, 512], F32))
        t_row = ec(nc.sbuf_tensor("t_row", [1, 512], F32))
        u_row = ec(nc.sbuf_tensor("u_row", [1, 512], F32))
        chmax = ec(nc.sbuf_tensor("chmax", [128, NDC], F32))
        wmax = ec(nc.sbuf_tensor("wmax", [128, NDC], F32))
        red1 = ec(nc.sbuf_tensor("red1", [128, 1], F32))
        red2 = ec(nc.sbuf_tensor("red2", [128, 1], F32))
        cc_s = ec(nc.sbuf_tensor("cc_s", [1, 8], F32))
        xs_t = ec(nc.sbuf_tensor("xs_t", [1, 1], F32))
        ws_t = ec(nc.sbuf_tensor("ws_t", [1, 1], F32))
        xs_bc = ec(nc.sbuf_tensor("xs_bc", [128, 1], F32))
        ws_bc = ec(nc.sbuf_tensor("ws_bc", [128, 1], F32))
        rxs = ec(nc.sbuf_tensor("rxs", [128, 1], F32))
        rws = ec(nc.sbuf_tensor("rws", [128, 1], F32))
        sy_bc = ec(nc.sbuf_tensor("sy_bc", [128, 1], F32))
        m_bc_s = ec(nc.sbuf_tensor("m_bc_s", [128, 512], F32))
        i_bc_s = ec(nc.sbuf_tensor("i_bc_s", [128, 512], F32))
        nmx_a = ec(nc.sbuf_tensor("nmx_a", [128, 16], F32))
        sum_a = ec(nc.sbuf_tensor("sum_a", [128, 16], F32))
        rec_a = ec(nc.sbuf_tensor("rec_a", [128, 16], F32))

        b.op("g", lambda E: E.memset(ones_col[:], 1.0))
        b.op("g", lambda E: E.memset(ones_row[:], 1.0))
        b.op("g", lambda E: E.memset(cc_s[:], 0.0))

        b.op("g", lambda E: E.memset(idn[:], 1.0))
        b.op("g", lambda E: E.affine_select(
            out=idn[:], in_=idn[:], compare_op=ALU.is_ge, fill=0.0, base=0,
            pattern=[[-1, 128]], channel_multiplier=1))
        e_id = b.op("g", lambda E: E.affine_select(
            out=idn[:], in_=idn[:], compare_op=ALU.is_ge, fill=0.0, base=0,
            pattern=[[1, 128]], channel_multiplier=-1))
        for qt in range(4):
            b.op("g", lambda E, qt=qt: E.memset(amask[qt][:], 0.0))
            b.op("g", lambda E, qt=qt: E.affine_select(
                out=amask[qt][:], in_=amask[qt][:], compare_op=ALU.is_ge,
                fill=-1e9, base=qt * 128, pattern=[[-1, 512]],
                channel_multiplier=1))
        e_masks = b.ev("G")
        e_idnb = b.op("v", lambda E: E.tensor_copy(idnb[:], idn[:]),
                      waits=[e_id], inc="V")

        def strided_bias(dst, src_ap, n):
            return b.dma("s", "d_bias", dst[:, 0:n],
                         bass.AP(src_ap.tensor, 0, [[1, 128], [128, n]]),
                         allow_slow_non_contiguous=True)
        strided_bias(cb_s, cb_h, NOC)
        strided_bias(bq_s, bq_h, NWC)
        strided_bias(bk_s, bk_h, NWC)
        strided_bias(bo_s, bo_h, NWC)
        strided_bias(pb_s, pb_h, NDC)
        b.dma("s", "d_bias", bv_row[:, :],
              bass.AP(bv_h.tensor, 0, [[1, 1], [1, 512]]))
        b.dma("g", "d_fb", fb_s[:, 0:NDC],
              bass.AP(fb_h.tensor, 0, [[1, 128], [128, NDC]]),
              allow_slow_non_contiguous=True)
        e_bias_dma = b.ev("d_bias")
        e_bv = e_bias_dma
        e_fb = b.ev("d_fb")
        e_bqs = b.op("v", lambda E: E.tensor_scalar_mul(bqs_s[:], bq_s[:], 0.125),
                     waits=[e_bias_dma], inc="V")
        sl0, bank0, rel0 = pf_rot.take()
        e_t0 = b.op("t", lambda E, p=bank0: E.matmul(
            p[:, 0:512], ones_row[0:1, :], bv_row[0:1, :], start=True, stop=True,
            skip_group_check=True), waits=[e_bv, rel0, e_masks], inc="T")
        e_bvbc = b.op("v", lambda E, p=bank0: E.tensor_copy(bv_bc[:], p[:, 0:512]),
                      waits=[e_t0], inc="V")
        pf_rot.release(sl0, e_bvbc)

        stkA = ExitStack()
        ecA = stkA.enter_context
        xtm = Rot([ecA(nc.sbuf_tensor(f"xtm{i}", [128, 2048], F32))
                   for i in range(2)])
        XW = TOK + 2
        xhb = ecA(nc.sbuf_tensor("xhb", [128, NDC * XW], BF16))
        xlb = ecA(nc.sbuf_tensor("xlb", [128, NDC * XW], BF16))

        def xh(dc, a, n):
            return xhb[:, dc * XW + a: dc * XW + a + n]

        def xl(dc, a, n):
            return xlb[:, dc * XW + a: dc * XW + a + n]
        NCW = 3
        cwf = Rot([ecA(nc.sbuf_tensor(f"cwf{i}", [128, 1536], F32))
                   for i in range(NCW)])
        cwh = [ecA(nc.sbuf_tensor(f"cwh{i}", [128, 1536], BF16))
               for i in range(NCW)]
        cwl = [ecA(nc.sbuf_tensor(f"cwl{i}", [128, 1536], BF16))
               for i in range(NCW)]
        cw_rel = [None] * NCW
        qkv_tmp = Rot([ecA(nc.sbuf_tensor(f"qtmp{i}", [128, 512], F32))
                       for i in range(2)])
        wt_stA = Rot([ecA(nc.sbuf_tensor(f"wtsa{i}", [128, 2048], BF16))
                      for i in range(2)])
        qkv_hi = Rot([ecA(nc.sbuf_tensor(f"qhi{i}", [128, 512], BF16))
                      for i in range(4)])
        qkv_lo = Rot([ecA(nc.sbuf_tensor(f"qlo{i}", [128, 512], BF16))
                      for i in range(4)])

        NGRP = NOC * 4
        gload = [None] * NGRP

        def load_group(k):
            oc, dcg = divmod(k, 4)
            csl, cbuf, crel = cwf.take()
            e_w = None
            for tap in range(3):
                src = bass.AP(cw_h.tensor,
                              tap * D * QKV + dcg * 4 * 128 * QKV + oc * 128,
                              [[QKV, 128], [128 * QKV, 4], [1, 128]])
                e_w = b.dma("s", f"d_cw{csl}",
                            cbuf[:, tap * 512:(tap + 1) * 512], src,
                            waits=[crel])
            e_sh = b.op("a", lambda E, cb_=cbuf, csl=csl: E.copy(
                cwh[csl][:], cb_[:]), waits=[e_w, cw_rel[csl]], inc="A")
            e_sl = b.op("v", lambda E, cb_=cbuf, csl=csl: E.tensor_tensor(
                cwl[csl][:], cb_[:], cwh[csl][:], ALU.subtract),
                waits=[e_w, e_sh])
            cwf.release(csl, e_sl)
            gload[k] = (csl, e_sh, e_sl)

        xdone_ev = [None] * NDC
        for t in range(9):
            rows = 2 if t == 8 else 128
            sl, xbuf, rel = xtm.take()
            e_ld = b.dma("s", f"d_xt{sl}", xbuf[0:rows, :],
                         x_h[t * 128:t * 128 + rows, :], waits=[rel])
            last = None
            col = t * 128
            for q4 in range(4):
                psl, pbank, prel = pf_rot.take()

                def tp4(E, p=pbank, xb=xbuf, q4=q4):
                    ins = None
                    for j in range(4):
                        dc = q4 * 4 + j
                        ins = E.transpose(p[:, j * 128:(j + 1) * 128],
                                          xb[:, dc * 128:(dc + 1) * 128],
                                          idn[:])
                    return ins
                e_tp = b.op("t", tp4, waits=[e_ld, prel, e_id], inc="T")
                dsth = bass.AP(xhb, (q4 * 4) * XW + col,
                               [[NDC * XW, 128], [XW, 4], [1, rows]])
                dstl = bass.AP(xlb, (q4 * 4) * XW + col,
                               [[NDC * XW, 128], [XW, 4], [1, rows]])
                srcp = bass.AP(pbank, 0, [[512, 128], [128, 4], [1, rows]])
                e_hi = b.op("a", lambda E, s=srcp, d=dsth: E.copy(d, s),
                            waits=[e_tp], inc="A")
                e_lo = b.op("v", lambda E, s=srcp, dh=dsth, dl=dstl:
                            E.tensor_tensor(dl, s, dh, ALU.subtract),
                            waits=[e_tp, e_hi])
                pf_rot.release(psl, e_lo)
                for j in range(4):
                    xdone_ev[q4 * 4 + j] = e_lo
                last = e_lo
            xtm.release(sl, last)
            if t < NCW:
                load_group(t)
        for h in range(4):
            cols = slice(h * 4 * XW, (h + 1) * 4 * XW)
            b.dma("g", "d_xscr", xh_scr[:, cols], xhb[:, cols],
                  waits=[xdone_ev[h * 4 + 3]])
            b.dma("g", "d_xscr", xl_scr[:, cols], xlb[:, cols],
                  waits=[xdone_ev[h * 4 + 3]])
        e_xscr = b.ev("d_xscr")

        wsred_evs = []
        for oc in range(NOC):
            gslot = []
            for tt in range(2):
                s_, bank_, rel_ = pf_rot.take()
                gslot.append((s_, bank_, rel_))
            e_mm_last = None
            for dcg in range(4):
                k = oc * 4 + dcg
                csl, e_sh, e_sl = gload[k]
                xw = [xdone_ev[dcg * 4 + i] for i in range(4)]
                banks = (gslot[0][1], gslot[1][1])
                first = dcg == 0

                def conv_mms(E, csl=csl, dcg=dcg, banks=banks, first=first):
                    ins = None
                    st = [first, first]
                    for di in range(4):
                        dc = dcg * 4 + di
                        for tap in range(3):
                            wcol = (tap * 4 + di) * 128
                            whi = cwh[csl][:, wcol:wcol + 128]
                            wlo = cwl[csl][:, wcol:wcol + 128]
                            for tt in range(2):
                                c0 = tt * 512 + tap
                                rh = xh(dc, c0, 512)
                                rl = xl(dc, c0, 512)
                                for lhsT, rhs in ((whi, rh), (whi, rl), (wlo, rh)):
                                    ins = E.matmul(banks[tt][:, 0:512], lhsT, rhs,
                                                   start=st[tt], stop=False,
                                                   skip_group_check=True)
                                    st[tt] = False
                    return ins
                waits = [e_sh, e_sl] + xw
                if dcg == 0:
                    waits += [gslot[0][2], gslot[1][2]]
                e_mm_last = b.op("t", conv_mms, waits=waits, inc="T")
                cw_rel[csl] = e_mm_last
                if k + NCW < NGRP:
                    load_group(k + NCW)
            if 4 <= oc < 12:
                for j in range(2):
                    kc = (oc - 4) * 2 + j
                    wsl, wbuf, wrel = wt_stA.take()
                    e_wl = b.dma("g", f"d_wA{wsl}", wbuf[:],
                                 fw_h[kc * 128:(kc + 1) * 128, :], waits=[wrel])
                    e_wr = b.op("v", lambda E, wbuf=wbuf, kc=kc: E.reduce_max(
                        wmax[:, kc:kc + 1], wbuf[:], axis=AX.X,
                        apply_absolute_value=True), waits=[e_wl])
                    wt_stA.release(wsl, e_wr)
                    wsred_evs.append(e_wr)
            for tt in range(2):
                s_, bank_, _ = gslot[tt]
                tsl, tmp, trel = qkv_tmp.take()
                e_e1 = b.op("v", lambda E, p=bank_, tmp=tmp, oc=oc:
                            E.tensor_scalar_add(tmp[:], p[:, 0:512],
                                                cb_s[:, oc:oc + 1]),
                            waits=[e_mm_last, trel, e_bias_dma])
                pf_rot.release(s_, e_e1)
                hsl, bh, hrel = qkv_hi.take()
                lsl, bl_, lrel = qkv_lo.take()
                e_h = b.op("v", lambda E, tmp=tmp, bh=bh: E.tensor_copy(
                    bh[:], tmp[:]), waits=[e_e1, hrel])
                e_l = b.op("g", lambda E, tmp=tmp, bh=bh, bl_=bl_:
                           E.tensor_tensor(bl_[:], tmp[:], bh[:], ALU.subtract),
                           waits=[e_h, lrel])
                qkv_tmp.release(tsl, e_l)
                eh_d = b.dma("g", f"d_qd{hsl}",
                             qkvh_scr[oc, :, tt * 512:(tt + 1) * 512], bh[:],
                             waits=[e_h])
                el_d = b.dma("g", f"d_ql{lsl}",
                             qkvl_scr[oc, :, tt * 512:(tt + 1) * 512], bl_[:],
                             waits=[e_l])
                qkv_hi.release(hsl, eh_d)
                qkv_lo.release(lsl, el_d)

        e_wm = b.op("v", lambda E: E.reduce_max(red2[:], wmax[:], axis=AX.X),
                    waits=wsred_evs, inc="V")
        sl2, bank2, rl2 = pf_rot.take()
        e_tp2 = b.op("t", lambda E, p=bank2: E.transpose(
            p[0:1, 0:128], red2[:], idn[:]),
            waits=[e_wm, rl2, e_id], inc="T")

        b.op("v", lambda E, p=bank2: E.reduce_max(
            t_row[0:1, 0:1], p[0:1, 0:128], axis=AX.X), waits=[e_tp2])
        e_ws = b.op("v", lambda E: E.tensor_scalar_max(
            ws_t[:], t_row[0:1, 0:1], 1e-6))
        pf_rot.release(sl2, e_ws)
        slw, bankw, rlw = pf_rot.take()
        e_bw = b.op("t", lambda E, p=bankw: E.matmul(
            p[:, 0:1], ones_row[0:1, :], ws_t[0:1, 0:1], start=True, stop=True,
            skip_group_check=True), waits=[e_ws, rlw, e_masks], inc="T")

        b.op("v", lambda E, p=bankw: E.tensor_copy(ws_bc[:], p[:, 0:1]),
             waits=[e_bw])
        e_rws = b.op("v", lambda E: E.reciprocal(rws[:], ws_bc[:]))
        pf_rot.release(slw, e_rws)


        conv_done = b.barrier_evs() + b.dma_evs()
        stkA.close()

        stkB = ExitStack()
        ecB = stkB.enter_context
        qin_h = [ecB(nc.sbuf_tensor(f"qinh{i}", [128, 512], BF16)) for i in range(NWC)]
        qin_l = [ecB(nc.sbuf_tensor(f"qinl{i}", [128, 512], BF16)) for i in range(NWC)]
        kin_h = [ecB(nc.sbuf_tensor(f"kinh{i}", [128, 512], BF16)) for i in range(NWC)]
        kin_l = [ecB(nc.sbuf_tensor(f"kinl{i}", [128, 512], BF16)) for i in range(NWC)]
        vin_h = [ecB(nc.sbuf_tensor(f"vinh{i}", [128, 512], BF16)) for i in range(NWC)]
        vin_l = [ecB(nc.sbuf_tensor(f"vinl{i}", [128, 512], BF16)) for i in range(NWC)]
        wsf = Rot([ecB(nc.sbuf_tensor(f"wsf{i}", [128, 512], F32)) for i in range(1)])
        wts = {}
        for nm in ("wq", "wk", "wv"):
            wts[nm] = ([ecB(nc.sbuf_tensor(f"{nm}h{i}", [128, 512], BF16))
                        for i in range(NWC)],
                       [ecB(nc.sbuf_tensor(f"{nm}l{i}", [128, 512], BF16))
                        for i in range(NWC)])
        wts["wo"] = wts["wq"]
        pwf = Rot([ecB(nc.sbuf_tensor(f"pwf{i}", [128, 1024], F32))
                   for i in range(1)])
        pw_hi = [ecB(nc.sbuf_tensor(f"pwh{i}", [128, 1024], BF16))
                 for i in range(NWC)]
        pw_lo = [ecB(nc.sbuf_tensor(f"pwl{i}", [128, 1024], BF16))
                 for i in range(NWC)]
        qh_hi = [ecB(nc.sbuf_tensor(f"qhh{i}", [128, 512], BF16)) for i in range(NWC)]
        qh_lo = [ecB(nc.sbuf_tensor(f"qhl{i}", [128, 512], BF16)) for i in range(NWC)]
        kh_hi = [ecB(nc.sbuf_tensor(f"khh{i}", [128, 512], BF16)) for i in range(NWC)]
        kh_lo = [ecB(nc.sbuf_tensor(f"khl{i}", [128, 512], BF16)) for i in range(NWC)]
        vt_hi = [ecB(nc.sbuf_tensor(f"vth{i}", [128, 512], BF16)) for i in range(NWC)]
        vt_lo = [ecB(nc.sbuf_tensor(f"vtl{i}", [128, 512], BF16)) for i in range(NWC)]
        hp_tmp = Rot([ecB(nc.sbuf_tensor(f"hpt{i}", [128, 512], F32))
                      for i in range(1)])
        mskd = Rot([ecB(nc.sbuf_tensor(f"mskd{i}", [128, 512], F32))
                    for i in range(1)])
        exr = Rot([ecB(nc.sbuf_tensor(f"exr{i}", [128, 512], F32))
                   for i in range(1)])
        pnr = Rot([ecB(nc.sbuf_tensor(f"pnr{i}", [128, 512], F32))
                   for i in range(2)])
        p_hi = [[ecB(nc.sbuf_tensor(f"phi{h}_{q}", [128, 512], BF16))
                 for q in range(4)] for h in range(2)]
        p_lo = [[ecB(nc.sbuf_tensor(f"plo{h}_{q}", [128, 512], BF16))
                 for q in range(4)] for h in range(2)]
        pTh = Rot([ecB(nc.sbuf_tensor(f"pth{i}", [128, 512], BF16))
                   for i in range(2)])
        pTl = Rot([ecB(nc.sbuf_tensor(f"ptl{i}", [128, 512], BF16))
                   for i in range(2)])
        ctx_hi = [ecB(nc.sbuf_tensor(f"cxh{i}", [128, 512], BF16))
                  for i in range(NPAIR)]
        ctx_lo = [ecB(nc.sbuf_tensor(f"cxl{i}", [128, 512], BF16))
                  for i in range(NPAIR)]
        mha_hi = [ecB(nc.sbuf_tensor(f"mhh{i}", [128, 512], BF16))
                  for i in range(NWC)]
        mha_lo = [ecB(nc.sbuf_tensor(f"mhl{i}", [128, 512], BF16))
                  for i in range(NWC)]
        xres = Rot([ecB(nc.sbuf_tensor(f"xres{i}", [128, 512], BF16))
                    for i in range(4)])

        rel_qin = [None] * NWC
        rel_kin = [None] * NWC
        rel_vin = [None] * NWC
        rel_w = {nm: [None] * NWC for nm in ("wq", "wk", "wv", "wo")}
        rel_pw = [None] * NWC
        rel_qh = [None] * NWC
        rel_kh = [None] * NWC
        rel_vt = [None] * NWC
        rel_ctx = [None] * NPAIR
        rel_mha = [None] * NWC
        rel_phil = [[None] * 4 for _ in range(2)]
        rel_x1s = [None] * NDC
        x1_out_ev = [[None] * NDC, [None] * NDC]
        x1_apply_ev = [[None] * NDC, [None] * NDC]
        nmx_i = [0]

        def split_f32(src, dst_h, dst_l, e_src, rels=(None, None), lo_eng="v"):
            e_h = b.op("v", lambda E, src=src, dst_h=dst_h: E.tensor_copy(
                dst_h[:], src[:]), waits=[e_src, rels[0]], inc="V")
            e_l = b.op(lo_eng, lambda E, src=src, dst_h=dst_h, dst_l=dst_l:
                       E.tensor_tensor(dst_l[:], src[:], dst_h[:], ALU.subtract),
                       waits=[e_h, rels[1]], inc=ENG_SEM[lo_eng])
            return e_h, e_l

        def emit_ln(src_tiles, apply_dst, src_evs, post_apply=None):
            e_stat = None
            for dc in range(NDC):
                ssl, sqt, srel = sqr.take()
                e_sq = b.op("v" if dc % 2 == 0 else "g",
                            lambda E, sqt=sqt, dc=dc, src=src_tiles:
                            E.tensor_mul(sqt[:], src[dc][:], src[dc][:]),
                            waits=[src_evs[dc], srel])

                def stat_mm(E, dc=dc, sqt=sqt, src=src_tiles):
                    E.matmul(PS0[0:1, :], ones_col[:, 0:1], src[dc][:],
                             start=(dc == 0), stop=(dc == NDC - 1),
                             skip_group_check=True)
                    return E.matmul(PS1[0:1, :], ones_col[:, 0:1], sqt[:],
                                    start=(dc == 0), stop=(dc == NDC - 1),
                                    skip_group_check=True)
                e_stat = b.op("t", stat_mm,
                              waits=[e_sq, src_evs[dc], ps_rel[0], e_masks],
                              inc="T")
                sqr.release(ssl, e_stat)
            inv_d = 1.0 / D
            b.op("v", lambda E: E.tensor_scalar_mul(m_row[:], PS0[0:1, :], inv_d),
                 waits=[e_stat])
            b.op("v", lambda E: E.tensor_scalar_mul(v_row[:], PS1[0:1, :], inv_d))
            b.op("v", lambda E: E.tensor_mul(t_row[:], m_row[:], m_row[:]))
            b.op("v", lambda E: E.tensor_sub(v_row[:], v_row[:], t_row[:]))
            e_r1 = b.op("v", lambda E: E.tensor_scalar_add(v_row[:], v_row[:],
                                                           EPS_LN))
            ps_rel[0] = e_r1
            e_sq2 = b.op("a", lambda E: E.activation(u_row[:], v_row[:], ACTF.Sqrt),
                         waits=[e_r1], inc="A")
            b.op("v", lambda E: E.reciprocal(i_row[:], u_row[:]), waits=[e_sq2])
            for _ in range(2):
                b.op("v", lambda E: E.tensor_mul(t_row[:], i_row[:], i_row[:]))
                b.op("v", lambda E: E.tensor_mul(t_row[:], t_row[:], v_row[:]))
                b.op("v", lambda E: E.tensor_scalar(t_row[:], t_row[:], -0.5, 1.5,
                                                    ALU.mult, ALU.add))
                e_r2 = b.op("v", lambda E: E.tensor_mul(i_row[:], i_row[:],
                                                        t_row[:]))
            sl_m, bank_m, rel_m = pf_rot.take()
            e_bm = b.op("t", lambda E, p=bank_m: E.matmul(
                p[:, 0:512], ones_row[0:1, :], m_row[0:1, :], start=True,
                stop=True, skip_group_check=True),
                waits=[e_r1, rel_m, e_masks], inc="T")
            sl_i, bank_i, rel_i = pf_rot.take()
            e_bi = b.op("t", lambda E, p=bank_i: E.matmul(
                p[:, 0:512], ones_row[0:1, :], i_row[0:1, :], start=True,
                stop=True, skip_group_check=True),
                waits=[e_r2, rel_i, e_masks], inc="T")
            e_cm = b.op("v", lambda E, p=bank_m: E.tensor_copy(
                m_bc_s[:], p[:, 0:512]), waits=[e_bm])
            e_ci = b.op("v", lambda E, p=bank_i: E.tensor_copy(
                i_bc_s[:], p[:, 0:512]), waits=[e_bi])
            pf_rot.release(sl_m, e_cm)
            pf_rot.release(sl_i, e_ci)
            apply_evs = []
            for dc in range(NDC):
                dst, wa = apply_dst(dc)
                eng = "v" if dc % 2 == 1 else "g"
                b.op(eng, lambda E, dc=dc, dst=dst, src=src_tiles:
                     E.tensor_tensor(dst[:], src[dc][:], m_bc_s[:],
                                     ALU.subtract),
                     waits=[e_cm, e_ci, src_evs[dc]] + list(wa))
                e_ap = b.op(eng, lambda E, dst=dst:
                            E.tensor_tensor(dst[:], dst[:], i_bc_s[:],
                                            ALU.mult))
                apply_evs.append(e_ap)
                if post_apply is not None:
                    post_apply(dc, e_ap)
            return apply_evs

        for blk in range(NB):
            c0 = blk * 512
            bw = conv_done if blk == 0 else []
            ein_q, ein_k, ein_v = [], [], []
            for wc in range(NWC):
                for (dsth, dstl, base, ein, relv) in (
                        (qin_h, qin_l, 0, ein_q, rel_qin),
                        (kin_h, kin_l, 4, ein_k, rel_kin),
                        (vin_h, vin_l, 8, ein_v, rel_vin)):
                    sname = {0: "d_qin", 4: "d_kin", 8: "d_vin"}[base]
                    e1 = b.dma("a", sname, dsth[wc][:],
                               qkvh_scr[base + wc, :, c0:c0 + 512],
                               waits=bw + [relv[wc]])
                    e2 = b.dma("a", sname, dstl[wc][:],
                               qkvl_scr[base + wc, :, c0:c0 + 512],
                               waits=[relv[wc]])
                    ein.append((e1, e2))
            qin_all = b.ev("d_qin")
            kin_all = b.ev("d_kin")
            vin_all = b.ev("d_vin")
            ein_q = [(qin_all, qin_all)] * NWC
            ein_k = [(kin_all, kin_all)] * NWC
            ein_v = [(vin_all, vin_all)] * NWC
            wsplit_ev = {}
            for name, h_ap in (("wq", wq_h), ("wk", wk_h), ("wv", wv_h)):
                dsth, dstl = wts[name]
                evs = []
                for wc in range(NWC):
                    ssl, sbuf, srel = wsf.take()
                    e_w = b.dma("a", "d_wsf", sbuf[:],
                                h_ap[wc * 128:(wc + 1) * 128, :],
                                waits=bw + [srel, rel_w[name][wc]])
                    e_h, e_l = split_f32(sbuf, dsth[wc], dstl[wc], e_w,
                                         rels=(rel_w[name][wc],
                                               rel_w[name][wc]),
                                         lo_eng="v" if wc % 2 == 0 else "g")
                    wsf.release(ssl, e_l)
                    evs.append((e_h, e_l))
                wsplit_ev[name] = evs

            def hp_group(dst_hi, dst_lo, in_h, in_l, ein, wname, bias_kind,
                         rel_dst, vt_mode=False):
                whi, wlo = wts[wname]
                out_evs = []
                for i in range(NWC):
                    psl, pbank, prel = pf_rot.take()

                    def mms(E, i=i, pbank=pbank, in_h=tuple(in_h),
                            in_l=tuple(in_l), whi=tuple(whi), wlo=tuple(wlo),
                            vt_mode=vt_mode):
                        ins = None
                        st = True
                        for wc in range(NWC):
                            if vt_mode:
                                l_h = in_h[wc][:, i * 128:(i + 1) * 128]
                                l_l = in_l[wc][:, i * 128:(i + 1) * 128]
                                r_h, r_l = whi[wc][:], wlo[wc][:]
                            else:
                                l_h = whi[wc][:, i * 128:(i + 1) * 128]
                                l_l = wlo[wc][:, i * 128:(i + 1) * 128]
                                r_h, r_l = in_h[wc][:], in_l[wc][:]
                            for lhsT, rhs in ((l_h, r_h), (l_l, r_h),
                                              (l_h, r_l)):
                                ins = E.matmul(pbank[:, 0:512], lhsT, rhs,
                                               start=st, stop=False,
                                               skip_group_check=True)
                                st = False
                        return ins
                    waits = [prel]
                    for wc in range(NWC):
                        waits += [ein[wc][0], ein[wc][1],
                                  wsplit_ev[wname][wc][0],
                                  wsplit_ev[wname][wc][1]]
                    e_mm = b.op("t", mms, waits=waits, inc="T")
                    tsl, tmp, trel = hp_tmp.take()
                    if bias_kind == "q":
                        e_e = b.op("v", lambda E, p=pbank, tmp=tmp, i=i:
                                   E.tensor_scalar(tmp[:], p[:, 0:512], 0.125,
                                                   bqs_s[:, i:i + 1],
                                                   ALU.mult, ALU.add),
                                   waits=[e_mm, trel, e_bqs], inc="V")
                    elif bias_kind == "k":
                        e_e = b.op("a", lambda E, p=pbank, tmp=tmp, i=i: E.add(
                            tmp[:], p[:, 0:512], bk_s[:, i:i + 1]),
                            waits=[e_mm, trel, e_bias_dma], inc="A")
                    else:
                        e_e = b.op("v", lambda E, p=pbank, tmp=tmp:
                                   E.tensor_tensor(tmp[:], p[:, 0:512],
                                                   bv_bc[:], ALU.add),
                                   waits=[e_mm, trel, e_bvbc], inc="V")
                    pf_rot.release(psl, e_e)
                    e_h, e_l = split_f32(tmp, dst_hi[i], dst_lo[i], e_e,
                                         rels=(rel_dst[i], rel_dst[i]),
                                         lo_eng="v" if i % 2 == 0 else "g")
                    hp_tmp.release(tsl, e_l)
                    out_evs.append((e_h, e_l))
                return out_evs
            eqh = hp_group(qh_hi, qh_lo, qin_h, qin_l, ein_q, "wq", "q", rel_qh)
            ekh = hp_group(kh_hi, kh_lo, kin_h, kin_l, ein_k, "wk", "k", rel_kh)
            evt = hp_group(vt_hi, vt_lo, vin_h, vin_l, ein_v, "wv", "v", rel_vt,
                           vt_mode=True)
            for wc in range(NWC):
                rel_qin[wc] = eqh[-1][1]
                rel_kin[wc] = ekh[-1][1]
                rel_vin[wc] = evt[-1][1]
            qh_ready = [e for pr in eqh for e in pr]
            kh_ready = [e for pr in ekh for e in pr]
            vt_ready = [e for pr in evt for e in pr]

            dsth, dstl = wts["wo"]
            evs = []
            for wc in range(NWC):
                ssl, sbuf, srel = wsf.take()
                e_w = b.dma("a", "d_wsf", sbuf[:],
                            wo_h[wc * 128:(wc + 1) * 128, :],
                            waits=[srel, rel_w["wo"][wc]] + qh_ready)
                e_h, e_l = split_f32(sbuf, dsth[wc], dstl[wc], e_w,
                                     rels=(rel_w["wo"][wc],
                                           rel_w["wo"][wc]),
                                     lo_eng="v" if wc % 2 == 0 else "g")
                wsf.release(ssl, e_l)
                evs.append((e_h, e_l))
            wsplit_ev["wo"] = evs

            def emit_pw_stream(half):
                evs = []
                for wc in range(NWC):
                    fsl, fbuf, frel = pwf.take()
                    e_w = b.dma("s", ("d_pwf0", "d_pwf1")[fsl], fbuf[:],
                                pw_h[wc * 128:(wc + 1) * 128,
                                     half * 1024:(half + 1) * 1024],
                                waits=[frel] + (bw if blk == 0 else []))
                    e_h = b.op("a", lambda E, fbuf=fbuf, wc=wc: E.copy(
                        pw_hi[wc][:], fbuf[:]),
                        waits=[e_w, rel_pw[wc]], inc="A")
                    e_l = b.op("v", lambda E, fbuf=fbuf, wc=wc: E.tensor_tensor(
                        pw_lo[wc][:], fbuf[:], pw_hi[wc][:], ALU.subtract),
                        waits=[e_h], inc="V")
                    pwf.release(fsl, e_l)
                    evs += [e_h, e_l]
                return evs
            pw_stream_evs = {}
            pw_stream_evs[0] = emit_pw_stream(0)

            def emit_smm(p, qts):
                out = []
                for qt in qts:
                    for hh in range(2):
                        psl, pbank, prel = pf_rot.take()
                        r0 = hh * 64

                        def smm(E, p_=p, qt=qt, r0=r0, pbank=pbank):
                            ins = None
                            st = True
                            qsl = slice(qt * 128, (qt + 1) * 128)
                            for lhsT, rhs in (
                                    (qh_hi[p_][r0:r0 + 64, qsl],
                                     kh_hi[p_][r0:r0 + 64, :]),
                                    (qh_lo[p_][r0:r0 + 64, qsl],
                                     kh_hi[p_][r0:r0 + 64, :]),
                                    (qh_hi[p_][r0:r0 + 64, qsl],
                                     kh_lo[p_][r0:r0 + 64, :])):
                                ins = E.matmul(pbank[:, 0:512], lhsT, rhs,
                                               start=st, stop=False,
                                               skip_group_check=True,
                                               tile_position=(r0, 0))
                                st = False
                            return ins
                        e_mm = b.op("t", smm,
                                    waits=[prel] + qh_ready + kh_ready,
                                    inc="T")
                        out.append((qt, hh, psl, pbank, e_mm))
                return out

            def emit_softmax(batch):
                for (qt, hh, psl, pbank, e_mm) in batch:
                    msl, mbuf, mrel = mskd.take()
                    ni = nmx_i[0] % 16
                    nmx_i[0] += 1
                    nmx = nmx_a[:, ni:ni + 1]
                    sums = sum_a[:, ni:ni + 1]
                    rec = rec_a[:, ni:ni + 1]
                    b.op("v", lambda E, pbank=pbank, mbuf=mbuf, qt=qt:
                         E.tensor_tensor(mbuf[:], pbank[:, 0:512],
                                         amask[qt][:], ALU.add),
                         waits=[e_mm, mrel, e_masks])
                    e_msk = b.op("v", lambda E, mbuf=mbuf, nmx=nmx:
                                 E.reduce_max(nmx, mbuf[:], axis=AX.X,
                                              negate=True))
                    pf_rot.release(psl, e_msk)
                    esl, ebuf, erel = exr.take()
                    e_exp = b.op("a", lambda E, mbuf=mbuf, ebuf=ebuf,
                                 nmx=nmx, sums=sums: E.activation(
                                     ebuf[:], mbuf[:], ACTF.Exp, bias=nmx,
                                     scale=1.0, accum_out=sums),
                                 waits=[e_msk, erel], inc="A")
                    mskd.release(msl, e_exp)
                    e_rc = b.op("v", lambda E, rec=rec, sums=sums:
                                E.reciprocal(rec, sums), waits=[e_exp])
                    nsl, pnb, nrel = pnr.take()
                    e_pm = b.op("a", lambda E, ebuf=ebuf, rec=rec, pnb=pnb:
                                E.mul(pnb[:], ebuf[:], rec),
                                waits=[e_rc, nrel], inc="A")
                    exr.release(esl, e_pm)
                    e_hi = b.op("v", lambda E, pnb=pnb, hh=hh, qt=qt:
                                E.tensor_copy(p_hi[hh][qt][:], pnb[:]),
                                waits=[e_pm, rel_phil[hh][qt]])
                    e_lo = b.op("g", lambda E, pnb=pnb, hh=hh, qt=qt:
                                E.tensor_tensor(p_lo[hh][qt][:], pnb[:],
                                                p_hi[hh][qt][:],
                                                ALU.subtract),
                                waits=[e_hi])
                    pnr.release(nsl, e_lo)
                    rel_phil[hh][qt] = e_lo

            def emit_tp_ctx(p):
                csl, cbank, crel = pf_rot.take()
                e_ctx_last = None
                for hh in range(2):
                    philo = [rel_phil[hh][q] for q in range(4)]
                    e_tp_last = None
                    for sc in range(4):
                        def tp_fn(E, hh=hh, sc=sc):
                            ins = None
                            for qt in range(4):
                                E.transpose(
                                    PT0[:, qt * 128:(qt + 1) * 128],
                                    p_hi[hh][qt][:, sc * 128:(sc + 1) * 128],
                                    idnb[:])
                                ins = E.transpose(
                                    PT1[:, qt * 128:(qt + 1) * 128],
                                    p_lo[hh][qt][:, sc * 128:(sc + 1) * 128],
                                    idnb[:])
                            return ins
                        e_tp = b.op("t", tp_fn,
                                    waits=philo + [pt_rel[0], pt_rel[1],
                                                   e_idnb], inc="T")
                        e_tp_last = e_tp
                        hsl_, hbuf_, hrel_ = pTh.take()
                        lsl_, lbuf_, lrel_ = pTl.take()
                        e_ph = b.op("a", lambda E, hbuf_=hbuf_: E.copy(
                            hbuf_[:], PT0[:, 0:512]),
                            waits=[e_tp, hrel_], inc="A")
                        e_pl = b.op("v", lambda E, lbuf_=lbuf_: E.tensor_copy(
                            lbuf_[:], PT1[:, 0:512]),
                            waits=[e_tp, lrel_])
                        pt_rel[0] = e_pl
                        pt_rel[1] = e_pl
                        vcol = p * 128 + hh * 64

                        def ctx_fn(E, hh=hh, sc=sc, vcol=vcol, cbank=cbank,
                                   hbuf_=hbuf_, lbuf_=lbuf_):
                            ins = None
                            st = (sc == 0)
                            for lhsT, rhs in (
                                    (vt_hi[sc][:, vcol:vcol + 64], hbuf_[:]),
                                    (vt_hi[sc][:, vcol:vcol + 64], lbuf_[:]),
                                    (vt_lo[sc][:, vcol:vcol + 64], hbuf_[:])):
                                ins = E.matmul(
                                    cbank[hh * 64:(hh + 1) * 64, 0:512],
                                    lhsT, rhs, start=st, stop=False,
                                    skip_group_check=True,
                                    tile_position=(0, hh * 64))
                                st = False
                            return ins
                        waits = [e_ph, e_pl] + vt_ready
                        if sc == 0 and hh == 0:
                            waits.append(crel)
                        e_ctx_last = b.op("t", ctx_fn, waits=waits, inc="T")
                        pTh.release(hsl_, e_ctx_last)
                        pTl.release(lsl_, e_ctx_last)
                    for qt in range(4):
                        rel_phil[hh][qt] = e_tp_last
                e_ch = b.op("a", lambda E, p=p, cbank=cbank: E.copy(
                    ctx_hi[p][:], cbank[:, 0:512]),
                    waits=[e_ctx_last, rel_ctx[p]], inc="A")
                e_cl = b.op("v", lambda E, p=p, cbank=cbank: E.tensor_tensor(
                    ctx_lo[p][:], cbank[:, 0:512], ctx_hi[p][:],
                    ALU.subtract), waits=[e_ch], inc="V")
                pf_rot.release(csl, e_cl)
                rel_ctx[p] = e_cl

            pend = None
            for p in range(NPAIR):
                for qt in range(4):
                    batch = emit_smm(p, (qt,))
                    if qt == 0 and pend is not None:
                        emit_tp_ctx(pend)
                    emit_softmax(batch)
                pend = p
            emit_tp_ctx(pend)

            ctx_ready = [rel_ctx[p] for p in range(NPAIR)]

            mha_evs = []
            woh, wol = wts["wo"]
            for oc in range(NWC):
                psl, pbank, prel = pf_rot.take()

                def womm(E, oc=oc, pbank=pbank, woh=tuple(woh), wol=tuple(wol)):
                    ins = None
                    st = True
                    for p in range(NPAIR):
                        osl = slice(oc * 128, (oc + 1) * 128)
                        for lhsT, rhs in ((woh[p][:, osl], ctx_hi[p][:]),
                                          (wol[p][:, osl], ctx_hi[p][:]),
                                          (woh[p][:, osl], ctx_lo[p][:])):
                            ins = E.matmul(pbank[:, 0:512], lhsT, rhs,
                                           start=st, stop=False,
                                           skip_group_check=True)
                            st = False
                    return ins
                waits = [prel] + ctx_ready
                for p in range(NPAIR):
                    waits += [wsplit_ev["wo"][p][0], wsplit_ev["wo"][p][1]]
                e_mm = b.op("t", womm, waits=waits, inc="T")
                tsl, tmp, trel = hp_tmp.take()
                e_e = b.op("a", lambda E, pbank=pbank, tmp=tmp, oc=oc: E.add(
                    tmp[:], pbank[:, 0:512], bo_s[:, oc:oc + 1]),
                    waits=[e_mm, trel], inc="A")
                pf_rot.release(psl, e_e)
                e_h, e_l = split_f32(tmp, mha_hi[oc], mha_lo[oc], e_e,
                                     rels=(rel_mha[oc], rel_mha[oc]),
                                     lo_eng="v" if oc % 2 == 0 else "g")
                hp_tmp.release(tsl, e_l)
                mha_evs += [e_h, e_l]
            for p in range(NPAIR):
                rel_ctx[p] = mha_evs[-1]

            x1pre_evs = [None] * NDC
            for half in range(2):
                pw_evs_half = pw_stream_evs[half]
                e_mm_half = None
                for dc in range(half * 8, half * 8 + 8):
                    rsl_h, rbh, rrel_h = xres.take()
                    rsl_l, rbl, rrel_l = xres.take()
                    xo = dc * (TOK + 2) + 2 + c0
                    e_rh = b.dma("s", f"d_xr{rsl_h}", rbh[:],
                                 xh_scr[:, xo:xo + 512],
                                 waits=[rrel_h, e_xscr])
                    e_rl = b.dma("s", f"d_xr{rsl_l}", rbl[:],
                                 xl_scr[:, xo:xo + 512],
                                 waits=[rrel_l])
                    psl, pbank, prel = pf_rot.take()
                    lcol = dc * 128 - half * 1024

                    def pjmm(E, lcol=lcol, pbank=pbank, rbh=rbh, rbl=rbl):
                        ins = None
                        st = True
                        dsl = slice(lcol, lcol + 128)
                        for wc in range(NWC):
                            for lhsT, rhs in (
                                    (pw_hi[wc][:, dsl], mha_hi[wc][:]),
                                    (pw_lo[wc][:, dsl], mha_hi[wc][:]),
                                    (pw_hi[wc][:, dsl], mha_lo[wc][:])):
                                ins = E.matmul(pbank[:, 0:512], lhsT, rhs,
                                               start=st, stop=False,
                                               skip_group_check=True)
                                st = False
                        E.matmul(pbank[:, 0:512], idnb[:], rbh[:], start=False,
                                 stop=False, skip_group_check=True)
                        return E.matmul(pbank[:, 0:512], idnb[:], rbl[:],
                                        start=False, stop=True,
                                        skip_group_check=True)
                    waits = [prel, e_rh, e_rl, e_idnb] + mha_evs + pw_evs_half
                    e_mm = b.op("t", pjmm, waits=waits, inc="T")
                    e_mm_half = e_mm
                    xres.release(rsl_h, e_mm)
                    xres.release(rsl_l, e_mm)
                    e_e = b.op("a", lambda E, pbank=pbank, dc=dc: E.add(
                        x1s[dc][:], pbank[:, 0:512], pb_s[:, dc:dc + 1]),
                        waits=[e_mm, rel_x1s[dc]], inc="A")
                    pf_rot.release(psl, e_e)
                    x1pre_evs[dc] = e_e
                for wc in range(NWC):
                    rel_pw[wc] = e_mm_half
                if half == 0:
                    pw_stream_evs[1] = emit_pw_stream(1)
            for wc in range(NWC):
                rel_mha[wc] = x1pre_evs[-1]
                rel_qh[wc] = x1pre_evs[-1]
                rel_kh[wc] = x1pre_evs[-1]
                rel_vt[wc] = x1pre_evs[-1]
                for nm in ("wq", "wk", "wv", "wo"):
                    rel_w[nm][wc] = x1pre_evs[-1]

            apply_evs = emit_ln(x1s, lambda dc: (x1s[dc], []), x1pre_evs)
            for dc in range(NDC):
                if blk == 0:
                    e_mx = b.op("v", lambda E, dc=dc: E.reduce_max(
                        chmax[:, dc:dc + 1], x1s[dc][:], axis=AX.X,
                        apply_absolute_value=True),
                        waits=[apply_evs[dc]], inc="V")
                else:
                    b.op("v", lambda E, dc=dc: E.reduce_max(
                        red1[:], x1s[dc][:], axis=AX.X,
                        apply_absolute_value=True), waits=[apply_evs[dc]])
                    e_mx = b.op("v", lambda E, dc=dc: E.tensor_max(
                        chmax[:, dc:dc + 1], chmax[:, dc:dc + 1], red1[:]))
                x1_apply_ev[blk][dc] = e_mx
                e_d = b.dma("g", "d_x1o", x1_scr[dc, :, c0:c0 + 512],
                            x1s[dc][:], waits=[apply_evs[dc]])
                x1_out_ev[blk][dc] = e_d
            x1o_all = b.ev("d_x1o")
            for dc in range(NDC):
                x1_out_ev[blk][dc] = x1o_all
                if blk == 0:
                    rel_x1s[dc] = x1o_all

        attn_done = b.barrier_evs() + b.dma_evs()
        stkB.close()

        stkC = ExitStack()
        ecC = stkC.enter_context
        wt_st = Rot([ecC(nc.sbuf_tensor(f"wtst{i}", [128, 2048], BF16))
                     for i in range(2)])
        wq8 = [ecC(nc.sbuf_tensor(f"wq8_{i}", [128, 2048], F8))
               for i in range(NDC)]
        xq8_2 = [[ecC(nc.sbuf_tensor(f"xq8_{s}_{i}", [128, 512], F8))
                  for i in range(NDC)] for s in range(2)]
        yr = Rot([ecC(nc.sbuf_tensor(f"y{i}", [128, 512], BF16))
                  for i in range(6)])
        xstg = Rot([ecC(nc.sbuf_tensor(f"xstg{i}", [128, 512], F32))
                    for i in range(2)])
        outr = Rot([ecC(nc.sbuf_tensor(f"or{i}", [128, 512], F32))
                    for i in range(4)])

        e_lm = b.op("v", lambda E: E.reduce_max(red1[:], chmax[:], axis=AX.X),
                    waits=[x1_apply_ev[1][dc] for dc in range(NDC)], inc="V")
        sl1, bank1, rl1 = pf_rot.take()
        e_tp1 = b.op("t", lambda E, p=bank1: E.transpose(
            p[0:1, 0:128], red1[:], idn[:]),
            waits=[e_lm, rl1, e_id], inc="T")
        e_xm = b.op("v", lambda E, p=bank1: E.reduce_max(
            cc_s[0:1, 0:1], p[0:1, 0:128], axis=AX.X),
            waits=[e_tp1, e_masks], inc="V")
        pf_rot.release(sl1, e_xm)
        e_ccw = b.dma("g", "d_ccd", cc_in.ap()[:, :], cc_s[:], waits=[e_xm])
        e_cc = b.op("g", lambda E: E.collective_compute(
            "AllReduce", ALU.max, replica_groups=[list(range(NCORE))],
            ins=[cc_in.ap().opt()], outs=[cc_out.ap().opt()]),
            waits=[e_ccw], inc="C")
        e_ccr = b.dma("g", "d_ccd", cc_s[:], cc_out.ap()[:, :], waits=[e_cc])
        wq8_evs = []
        for kc in range(NDC):
            ssl, sbuf, srel = wt_st.take()
            e_w = b.dma("s", f"d_wt{ssl}", sbuf[:],
                        fw_h[:, kc * 128:(kc + 1) * 128],
                        waits=attn_done + [srel], transpose=True)
            e_q = b.op("v", lambda E, sbuf=sbuf, kc=kc: E.tensor_scalar(
                wq8[kc][:], sbuf[:], rws[:, 0:1], None, ALU.mult),
                waits=[e_w, e_rws])
            wt_st.release(ssl, e_q)
            wq8_evs.append(e_q)

        e_xs = b.op("v", lambda E: E.tensor_scalar_max(
            xs_t[:], cc_s[0:1, 0:1], 1e-6), waits=[e_ccr], inc="V")
        slx, bankx, rlx = pf_rot.take()
        e_bx = b.op("t", lambda E, p=bankx: E.matmul(
            p[:, 0:1], ones_row[0:1, :], xs_t[0:1, 0:1], start=True, stop=True,
            skip_group_check=True), waits=[e_xs, rlx, e_masks], inc="T")

        b.op("v", lambda E, p=bankx: E.tensor_copy(xs_bc[:], p[:, 0:1]),
             waits=[e_bx])
        e_rxs = b.op("v", lambda E: E.reciprocal(rxs[:], xs_bc[:]))
        pf_rot.release(slx, e_rxs)

        e_sy = b.op("v", lambda E: E.tensor_mul(sy_bc[:], xs_bc[:], ws_bc[:]),
                    waits=[e_rws, e_rxs])

        xq0_evs = []
        for dc in range(NDC):
            xsl, xbuf2, xrel2 = xstg.take()
            e_r = b.dma("s", f"d_xs{xsl}", xbuf2[:],
                        x1_scr[dc, :, 0:512], waits=[xrel2])
            e_q = b.op("v", lambda E, dc=dc, xbuf2=xbuf2:
                       E.tensor_scalar(xq8_2[0][dc][:], xbuf2[:],
                                       rxs[:, 0:1], None, ALU.mult),
                       waits=[e_r, e_rxs], inc="V")
            xstg.release(xsl, e_q)
            xq0_evs.append(e_q)

        ffn_mm_done = [None]
        for tt in (1, 0):
            c0 = tt * 512
            xq8 = xq8_2[tt]
            xq_evs = []
            if tt == 1:
                x1_evs = [x1_out_ev[1][dc] for dc in range(NDC)]
                for dc in range(NDC):
                    e_q = b.op("v", lambda E, dc=dc, xq8=xq8: E.tensor_scalar(
                        xq8[dc][:], x1s[dc][:], rxs[:, 0:1], None, ALU.mult),
                        waits=[x1_evs[dc], e_rxs], inc="V")
                    xq_evs.append(e_q)
            else:
                xq_evs = xq0_evs
                for dc in range(NDC):
                    b.dma("s", "d_x1r", x1s[dc][:], x1_scr[dc, :, 0:512],
                          waits=[rel_x1s[dc]])
                x1_evs = [b.ev("d_x1r")] * NDC
            x2_evs = [None] * NDC
            for oc_ in range(NDC):
                psl, pbank, prel = pf_rot.take()

                def fmm(E, oc_=oc_, pbank=pbank, xq8=xq8):
                    ins = None
                    for kc in range(NDC):
                        ins = E.matmul(pbank[:, 0:512],
                                       wq8[kc][:, oc_ * 128:(oc_ + 1) * 128],
                                       xq8[kc][:], start=(kc == 0),
                                       stop=(kc == NDC - 1),
                                       skip_group_check=True)
                    return ins
                e_mm = b.op("t", fmm, waits=[prel] + xq_evs + wq8_evs, inc="T")
                ysl, ybuf, yrel = yr.take()
                y2sl, y2buf, y2rel = yr.take()
                e_y = b.op("v", lambda E, pbank=pbank, ybuf=ybuf:
                           E.tensor_scalar(ybuf[:], pbank[:, 0:512],
                                           sy_bc[:, 0:1], None, ALU.mult),
                           waits=[e_mm, yrel, e_sy], inc="V")
                pf_rot.release(psl, e_y)
                e_la = b.op("g", lambda E, ybuf=ybuf, y2buf=y2buf:
                            E.tensor_copy(y2buf[:].bitcast(U16),
                                          ybuf[:].bitcast(U16)),
                            waits=[e_y, y2rel])
                yr.release(ysl, e_la)

                e_yb = b.op("v", lambda E, y2buf=y2buf, oc_=oc_: E.tensor_scalar(
                    y2buf[:], y2buf[:], fb_s[:, oc_:oc_ + 1], None, ALU.add),
                    waits=[e_la, e_fb, xq_evs[oc_]])
                e_x2 = b.op("g", lambda E, y2buf=y2buf, oc_=oc_:
                            E.tensor_tensor(x1s[oc_][:], y2buf[:],
                                            x1s[oc_][:], ALU.add),
                            waits=[e_yb, x1_evs[oc_]])
                yr.release(y2sl, e_x2)
                x2_evs[oc_] = e_x2
            ffn_mm_done[0] = b.ev("T")
            out_slots = {}

            def apply_dst(dc):
                osl_, obuf, orel = outr.take()
                out_slots[dc] = (obuf, osl_)
                return obuf, [orel]

            def post_apply(dc, e_ap, c0=c0, tt=tt):
                obuf, osl_ = out_slots[dc]
                e_o = b.dma("g", f"d_ot{osl_}",
                            out_h[dc * 128:(dc + 1) * 128, c0:c0 + 512],
                            obuf[:], waits=[e_ap])
                outr.release(osl_, e_o)
                if tt == 1:
                    rel_x1s[dc] = e_ap
            ln2_evs = emit_ln(x1s, apply_dst, x2_evs, post_apply=post_apply)

        b.op("s", lambda E: E.nop(), waits=b.dma_evs())
        b.op("g", lambda E: E.nop(), waits=b.dma_evs())
        stkC.close()

        block = top.enter_context(nc.Block())

        @block.tensor
        def _(e):
            for f in b.q["t"]:
                f(e)

        @block.vector
        def _(e):
            for f in b.q["v"]:
                f(e)

        @block.scalar
        def _(e):
            for f in b.q["a"]:
                f(e)

        @block.gpsimd
        def _(e):
            for f in b.q["g"]:
                f(e)

        @block.sync
        def _(e):
            for f in b.q["s"]:
                f(e)

    return nc


_CACHE = {}


def kernel(**inputs):
    x = np.asarray(inputs["L0_attn_x"])[0]
    cw = np.ascontiguousarray(inputs["L0_attn_conv_w"])
    cb = np.ascontiguousarray(inputs["L0_attn_conv_b"])
    wq = np.ascontiguousarray(np.asarray(inputs["L0_attn_wq"]).reshape(W, W))
    wk = np.ascontiguousarray(np.asarray(inputs["L0_attn_wk"]).reshape(W, W))
    wv = np.ascontiguousarray(np.asarray(inputs["L0_attn_wv"]).reshape(W, W))
    bq = np.ascontiguousarray(np.asarray(inputs["L0_attn_bq"]).reshape(W))
    bk = np.ascontiguousarray(np.asarray(inputs["L0_attn_bk"]).reshape(W))
    bv = np.ascontiguousarray(np.asarray(inputs["L0_attn_bv"]).reshape(W))
    wo = np.ascontiguousarray(np.asarray(inputs["L0_attn_wo"]).reshape(W, W))
    bo = np.ascontiguousarray(inputs["L0_attn_bo"])
    pw = np.ascontiguousarray(inputs["L0_attn_proj_w"])
    pb = np.ascontiguousarray(inputs["L0_attn_proj_b"])
    fw = np.ascontiguousarray(inputs["L0_ffn_weight"])
    fb = np.ascontiguousarray(inputs["L0_ffn_bias"])

    if "nc" not in _CACHE:
        _CACHE["nc"] = build_program()
    nc = _CACHE["nc"]

    in_maps = []
    for c in range(NCORE):
        xs = np.zeros((TOK + 2, D), dtype=np.float32)
        lo = c * TOK
        xs[2:] = x[lo:lo + TOK]
        if c > 0:
            xs[0:2] = x[lo - 2:lo]
        in_maps.append({
            "x": xs, "conv_w": cw, "conv_b": cb,
            "wq": wq, "wk": wk, "wv": wv, "bq": bq, "bk": bk, "bv": bv,
            "wo": wo, "bo": bo, "proj_w": pw, "proj_b": pb,
            "ffn_w": fw, "ffn_b": fb,
        })
    res = run_bass_kernel_spmd(nc, in_maps, core_ids=list(range(NCORE)))
    full = np.concatenate([res.results[c]["out"] for c in range(NCORE)], axis=1)
    return np.ascontiguousarray(full.T)[None].astype(np.float32)


# version 25
# speedup vs baseline: 1.0813x; 1.0786x over previous
import numpy as np
from contextlib import ExitStack

import concourse.bass as bass
import concourse.mybir as mybir
from concourse.bass_utils import run_bass_kernel_spmd

F32 = mybir.dt.float32
BF16 = mybir.dt.bfloat16
F8 = mybir.dt.float8e4
U16 = mybir.dt.uint16
AX = mybir.AxisListType
ALU = mybir.AluOpType
ACTF = mybir.ActivationFunctionType

NCORE = 8
TOK = 1024
NB = 2
D = 2048
NDC = D // 128
QKV = 1536
NOC = QKV // 128
W = 512
NWC = W // 128
NPAIR = 4
EPS_LN = 1e-5
ENG_SEM = {"v": "V", "g": "G", "a": "A", "t": "T"}


class Bld:
    def __init__(self, nc):
        self.nc = nc
        self.q = {e: [] for e in "tvags"}
        self.sems = {}
        self.cnt = {}
        self.hwm = {e: {} for e in "tvags"}

    def mksem(self, stk, name):
        self.sems[name] = stk.enter_context(self.nc.semaphore(name))
        self.cnt[name] = 0

    def ev(self, name):
        return (name, self.cnt[name])

    def op(self, eng, fn, waits=(), inc=None, amt=1):
        if eng in "vag":
            waits = list(waits)
            own = ENG_SEM[eng]
            if self.cnt[own] > 0:
                waits.append((own, self.cnt[own]))
            if inc is None:
                inc, amt = own, 1
        wl = []
        for wv in waits:
            if wv is None:
                continue
            s, v = wv
            if v <= 0:
                continue
            h = self.hwm[eng]
            if h.get(s, 0) >= v:
                continue
            h[s] = v
            wl.append((s, v))
        if inc is not None:
            self.cnt[inc] += amt
        sems = self.sems

        def run(E, wl=wl, fn=fn, inc=inc, amt=amt):
            for s, v in wl:
                E.wait_ge(sems[s], v)
            ins = fn(E)
            if inc is not None:
                ins.then_inc(sems[inc], amt)
        self.q[eng].append(run)
        return (inc, self.cnt[inc]) if inc is not None else None

    def dma(self, qeng, sem, out, in_, waits=(), **kw):
        return self.op(qeng, lambda E: E.dma_start(out=out, in_=in_, **kw),
                       waits, inc=sem, amt=16)

    def dma_evs(self):
        return [self.ev(s) for s in self.cnt if s.startswith("d_")]

    def barrier_evs(self):
        return [self.ev(s) for s in ("T", "V", "A", "G")]


class Rot:
    def __init__(self, items):
        self.items = list(items)
        self.rel = [None] * len(self.items)
        self.i = 0

    def take(self):
        s = self.i % len(self.items)
        self.i += 1
        return s, self.items[s], self.rel[s]

    def release(self, s, ev):
        self.rel[s] = ev


def build_program():
    nc = bass.Bass("TRN2", target_bir_lowering=False)

    x_h = nc.dram_tensor("x", [TOK + 2, D], F32, kind="ExternalInput").ap()
    cw_h = nc.dram_tensor("conv_w", [3, D, QKV], F32, kind="ExternalInput").ap()
    cb_h = nc.dram_tensor("conv_b", [QKV], F32, kind="ExternalInput").ap()
    wq_h = nc.dram_tensor("wq", [W, W], F32, kind="ExternalInput").ap()
    wk_h = nc.dram_tensor("wk", [W, W], F32, kind="ExternalInput").ap()
    wv_h = nc.dram_tensor("wv", [W, W], F32, kind="ExternalInput").ap()
    bq_h = nc.dram_tensor("bq", [W], F32, kind="ExternalInput").ap()
    bk_h = nc.dram_tensor("bk", [W], F32, kind="ExternalInput").ap()
    bv_h = nc.dram_tensor("bv", [W], F32, kind="ExternalInput").ap()
    wo_h = nc.dram_tensor("wo", [W, W], F32, kind="ExternalInput").ap()
    bo_h = nc.dram_tensor("bo", [W], F32, kind="ExternalInput").ap()
    pw_h = nc.dram_tensor("proj_w", [W, D], F32, kind="ExternalInput").ap()
    pb_h = nc.dram_tensor("proj_b", [D], F32, kind="ExternalInput").ap()
    fw_h = nc.dram_tensor("ffn_w", [D, D], BF16, kind="ExternalInput").ap()
    fb_h = nc.dram_tensor("ffn_b", [D], BF16, kind="ExternalInput").ap()
    out_h = nc.dram_tensor("out", [D, TOK], F32, kind="ExternalOutput").ap()

    xh_scr = nc.dram_tensor("xh_scr", [128, NDC * (TOK + 2)], BF16).ap()
    xl_scr = nc.dram_tensor("xl_scr", [128, NDC * (TOK + 2)], BF16).ap()
    qkvh_scr = nc.dram_tensor("qkvh_scr", [NOC, 128, TOK], BF16).ap()
    qkvl_scr = nc.dram_tensor("qkvl_scr", [NOC, 128, TOK], BF16).ap()
    x1_scr = nc.dram_tensor("x1_scr", [NDC, 128, TOK], F32).ap()
    cc_in = nc.dram_tensor("cc_in", [1, 8], F32)
    cc_out = nc.dram_tensor("cc_out", [1, 8], F32)

    b = Bld(nc)

    with ExitStack() as top:
        for s in (["T", "V", "A", "G", "C"]
                  + ["d_bias", "d_xt0", "d_xt1", "d_cw0", "d_cw1", "d_cw2",
                     "d_qd0", "d_qd1", "d_qd2", "d_qd3",
                     "d_ql0", "d_ql1", "d_ql2", "d_ql3",
                     "d_qin", "d_kin", "d_vin", "d_wsf", "d_pwf0", "d_pwf1",
                     "d_xr0", "d_xr1", "d_xr2", "d_xr3",
                     "d_x1o", "d_x1r", "d_wt0", "d_wt1",
                     "d_ot0", "d_ot1", "d_ot2", "d_ot3",
                     "d_ccd", "d_xscr", "d_fb", "d_wA0", "d_wA1", "d_xs0", "d_xs1"]):
            b.mksem(top, s)
        ec = top.enter_context

        PF = [ec(nc.psum_tensor(f"pf{i}", [128, 512], F32)) for i in range(3)]
        CTXB = ec(nc.psum_tensor("ctxb", [128, 512], F32))
        PT0 = ec(nc.psum_tensor("pt0", [128, 1024], BF16))
        PT1 = ec(nc.psum_tensor("pt1", [128, 1024], BF16))
        PS0 = ec(nc.psum_tensor("ps0", [1, 512], F32))
        PS1 = ec(nc.psum_tensor("ps1", [1, 512], F32))
        pf_rot = Rot(PF + [CTXB])
        pt_rel = [[], []]
        ps_rel = [None]

        idn = ec(nc.sbuf_tensor("idn", [128, 128], F32))
        idnb = ec(nc.sbuf_tensor("idnb", [128, 128], BF16))
        ones_col = ec(nc.sbuf_tensor("ones_col", [128, 1], F32))
        ones_row = ec(nc.sbuf_tensor("ones_row", [1, 128], F32))
        amask = [ec(nc.sbuf_tensor(f"amask{i}", [128, 512], BF16)) for i in range(4)]
        cb_s = ec(nc.sbuf_tensor("cb_s", [128, NOC], F32))
        bqs_s = ec(nc.sbuf_tensor("bqs_s", [128, NWC], F32))
        bq_s = ec(nc.sbuf_tensor("bq_s", [128, NWC], F32))
        bk_s = ec(nc.sbuf_tensor("bk_s", [128, NWC], F32))
        bo_s = ec(nc.sbuf_tensor("bo_s", [128, NWC], F32))
        pb_s = ec(nc.sbuf_tensor("pb_s", [128, NDC], F32))
        fb_s = ec(nc.sbuf_tensor("fb_s", [128, NDC], F32))
        bv_row = ec(nc.sbuf_tensor("bv_row", [1, 512], F32))
        bv_bc = ec(nc.sbuf_tensor("bv_bc", [128, 512], F32))
        x1s = [ec(nc.sbuf_tensor(f"x1s{i}", [128, 512], F32)) for i in range(NDC)]
        sqr = Rot([ec(nc.sbuf_tensor(f"sq{i}", [128, 512], F32)) for i in range(2)])
        m_row = ec(nc.sbuf_tensor("m_row", [1, 512], F32))
        v_row = ec(nc.sbuf_tensor("v_row", [1, 512], F32))
        i_row = ec(nc.sbuf_tensor("i_row", [1, 512], F32))
        t_row = ec(nc.sbuf_tensor("t_row", [1, 512], F32))
        u_row = ec(nc.sbuf_tensor("u_row", [1, 512], F32))
        chmax = ec(nc.sbuf_tensor("chmax", [128, NDC], F32))
        wmax = ec(nc.sbuf_tensor("wmax", [128, NDC], F32))
        red1 = ec(nc.sbuf_tensor("red1", [128, 1], F32))
        red2 = ec(nc.sbuf_tensor("red2", [128, 1], F32))
        cc_s = ec(nc.sbuf_tensor("cc_s", [1, 8], F32))
        xs_t = ec(nc.sbuf_tensor("xs_t", [1, 1], F32))
        ws_t = ec(nc.sbuf_tensor("ws_t", [1, 1], F32))
        xs_bc = ec(nc.sbuf_tensor("xs_bc", [128, 1], F32))
        ws_bc = ec(nc.sbuf_tensor("ws_bc", [128, 1], F32))
        rxs = ec(nc.sbuf_tensor("rxs", [128, 1], F32))
        rws = ec(nc.sbuf_tensor("rws", [128, 1], F32))
        sy_bc = ec(nc.sbuf_tensor("sy_bc", [128, 1], F32))
        m_bc_s = ec(nc.sbuf_tensor("m_bc_s", [128, 512], F32))
        i_bc_s = ec(nc.sbuf_tensor("i_bc_s", [128, 512], F32))
        nmx_a = ec(nc.sbuf_tensor("nmx_a", [128, 16], F32))
        sum_a = ec(nc.sbuf_tensor("sum_a", [128, 16], F32))
        rec_a = ec(nc.sbuf_tensor("rec_a", [128, 16], F32))

        b.op("g", lambda E: E.memset(ones_col[:], 1.0))
        b.op("g", lambda E: E.memset(ones_row[:], 1.0))
        b.op("g", lambda E: E.memset(cc_s[:], 0.0))

        b.op("g", lambda E: E.memset(idn[:], 1.0))
        b.op("g", lambda E: E.affine_select(
            out=idn[:], in_=idn[:], compare_op=ALU.is_ge, fill=0.0, base=0,
            pattern=[[-1, 128]], channel_multiplier=1))
        e_id = b.op("g", lambda E: E.affine_select(
            out=idn[:], in_=idn[:], compare_op=ALU.is_ge, fill=0.0, base=0,
            pattern=[[1, 128]], channel_multiplier=-1))
        for qt in range(4):
            b.op("g", lambda E, qt=qt: E.memset(amask[qt][:], 0.0))
            b.op("g", lambda E, qt=qt: E.affine_select(
                out=amask[qt][:], in_=amask[qt][:], compare_op=ALU.is_ge,
                fill=-1e9, base=qt * 128, pattern=[[-1, 512]],
                channel_multiplier=1))
        e_masks = b.ev("G")
        e_idnb = b.op("v", lambda E: E.tensor_copy(idnb[:], idn[:]),
                      waits=[e_id], inc="V")

        def strided_bias(dst, src_ap, n):
            return b.dma("s", "d_bias", dst[:, 0:n],
                         bass.AP(src_ap.tensor, 0, [[1, 128], [128, n]]),
                         allow_slow_non_contiguous=True)
        strided_bias(cb_s, cb_h, NOC)
        strided_bias(bq_s, bq_h, NWC)
        strided_bias(bk_s, bk_h, NWC)
        strided_bias(bo_s, bo_h, NWC)
        strided_bias(pb_s, pb_h, NDC)
        b.dma("s", "d_bias", bv_row[:, :],
              bass.AP(bv_h.tensor, 0, [[1, 1], [1, 512]]))
        b.dma("g", "d_fb", fb_s[:, 0:NDC],
              bass.AP(fb_h.tensor, 0, [[1, 128], [128, NDC]]),
              allow_slow_non_contiguous=True)
        e_bias_dma = b.ev("d_bias")
        e_bv = e_bias_dma
        e_fb = b.ev("d_fb")
        e_bqs = b.op("v", lambda E: E.tensor_scalar_mul(bqs_s[:], bq_s[:], 0.125),
                     waits=[e_bias_dma], inc="V")
        sl0, bank0, rel0 = pf_rot.take()
        e_t0 = b.op("t", lambda E, p=bank0: E.matmul(
            p[:, 0:512], ones_row[0:1, :], bv_row[0:1, :], start=True, stop=True,
            skip_group_check=True), waits=[e_bv, rel0, e_masks], inc="T")
        e_bvbc = b.op("v", lambda E, p=bank0: E.tensor_copy(bv_bc[:], p[:, 0:512]),
                      waits=[e_t0], inc="V")
        pf_rot.release(sl0, e_bvbc)

        stkA = ExitStack()
        ecA = stkA.enter_context
        xtm = Rot([ecA(nc.sbuf_tensor(f"xtm{i}", [128, 2048], F32))
                   for i in range(2)])
        XW = TOK + 2
        xhb = ecA(nc.sbuf_tensor("xhb", [128, NDC * XW], BF16))
        xlb = ecA(nc.sbuf_tensor("xlb", [128, NDC * XW], BF16))

        def xh(dc, a, n):
            return xhb[:, dc * XW + a: dc * XW + a + n]

        def xl(dc, a, n):
            return xlb[:, dc * XW + a: dc * XW + a + n]
        NCW = 3
        cwf = Rot([ecA(nc.sbuf_tensor(f"cwf{i}", [128, 1536], F32))
                   for i in range(NCW)])
        cwh = [ecA(nc.sbuf_tensor(f"cwh{i}", [128, 1536], BF16))
               for i in range(NCW)]
        cwl = [ecA(nc.sbuf_tensor(f"cwl{i}", [128, 1536], BF16))
               for i in range(NCW)]
        cw_rel = [None] * NCW
        qkv_tmp = Rot([ecA(nc.sbuf_tensor(f"qtmp{i}", [128, 512], F32))
                       for i in range(2)])
        wt_stA = Rot([ecA(nc.sbuf_tensor(f"wtsa{i}", [128, 2048], BF16))
                      for i in range(2)])
        qkv_hi = Rot([ecA(nc.sbuf_tensor(f"qhi{i}", [128, 512], BF16))
                      for i in range(4)])
        qkv_lo = Rot([ecA(nc.sbuf_tensor(f"qlo{i}", [128, 512], BF16))
                      for i in range(4)])

        NGRP = NOC * 4
        gload = [None] * NGRP

        def load_group(k):
            oc, dcg = divmod(k, 4)
            csl, cbuf, crel = cwf.take()
            e_w = None
            for tap in range(3):
                src = bass.AP(cw_h.tensor,
                              tap * D * QKV + dcg * 4 * 128 * QKV + oc * 128,
                              [[QKV, 128], [128 * QKV, 4], [1, 128]])
                e_w = b.dma("s", f"d_cw{csl}",
                            cbuf[:, tap * 512:(tap + 1) * 512], src,
                            waits=[crel])
            e_sh = b.op("a", lambda E, cb_=cbuf, csl=csl: E.copy(
                cwh[csl][:], cb_[:]), waits=[e_w, cw_rel[csl]], inc="A")
            e_sl = b.op("v", lambda E, cb_=cbuf, csl=csl: E.tensor_tensor(
                cwl[csl][:], cb_[:], cwh[csl][:], ALU.subtract),
                waits=[e_w, e_sh])
            cwf.release(csl, e_sl)
            gload[k] = (csl, e_sh, e_sl)

        xdone_ev = [None] * NDC
        for t in range(9):
            rows = 2 if t == 8 else 128
            sl, xbuf, rel = xtm.take()
            e_ld = b.dma("s", f"d_xt{sl}", xbuf[0:rows, :],
                         x_h[t * 128:t * 128 + rows, :], waits=[rel])
            last = None
            col = t * 128
            for q4 in range(4):
                psl, pbank, prel = pf_rot.take()

                def tp4(E, p=pbank, xb=xbuf, q4=q4):
                    ins = None
                    for j in range(4):
                        dc = q4 * 4 + j
                        ins = E.transpose(p[:, j * 128:(j + 1) * 128],
                                          xb[:, dc * 128:(dc + 1) * 128],
                                          idn[:])
                    return ins
                e_tp = b.op("t", tp4, waits=[e_ld, prel, e_id], inc="T")
                dsth = bass.AP(xhb, (q4 * 4) * XW + col,
                               [[NDC * XW, 128], [XW, 4], [1, rows]])
                dstl = bass.AP(xlb, (q4 * 4) * XW + col,
                               [[NDC * XW, 128], [XW, 4], [1, rows]])
                srcp = bass.AP(pbank, 0, [[512, 128], [128, 4], [1, rows]])
                e_hi = b.op("a", lambda E, s=srcp, d=dsth: E.copy(d, s),
                            waits=[e_tp], inc="A")
                e_lo = b.op("v", lambda E, s=srcp, dh=dsth, dl=dstl:
                            E.tensor_tensor(dl, s, dh, ALU.subtract),
                            waits=[e_tp, e_hi])
                pf_rot.release(psl, e_lo)
                for j in range(4):
                    xdone_ev[q4 * 4 + j] = e_lo
                last = e_lo
            xtm.release(sl, last)
            if t < NCW:
                load_group(t)
        for h in range(4):
            cols = slice(h * 4 * XW, (h + 1) * 4 * XW)
            b.dma("g", "d_xscr", xh_scr[:, cols], xhb[:, cols],
                  waits=[xdone_ev[h * 4 + 3]])
            b.dma("g", "d_xscr", xl_scr[:, cols], xlb[:, cols],
                  waits=[xdone_ev[h * 4 + 3]])
        e_xscr = b.ev("d_xscr")

        wsred_evs = []
        for oc in range(NOC):
            gslot = []
            for tt in range(2):
                s_, bank_, rel_ = pf_rot.take()
                gslot.append((s_, bank_, rel_))
            e_mm_last = None
            for dcg in range(4):
                k = oc * 4 + dcg
                csl, e_sh, e_sl = gload[k]
                xw = [xdone_ev[dcg * 4 + i] for i in range(4)]
                banks = (gslot[0][1], gslot[1][1])
                first = dcg == 0

                def conv_mms(E, csl=csl, dcg=dcg, banks=banks, first=first):
                    ins = None
                    st = [first, first]
                    for di in range(4):
                        dc = dcg * 4 + di
                        for tap in range(3):
                            wcol = (tap * 4 + di) * 128
                            whi = cwh[csl][:, wcol:wcol + 128]
                            wlo = cwl[csl][:, wcol:wcol + 128]
                            for tt in range(2):
                                c0 = tt * 512 + tap
                                rh = xh(dc, c0, 512)
                                rl = xl(dc, c0, 512)
                                for lhsT, rhs in ((whi, rh), (whi, rl), (wlo, rh)):
                                    ins = E.matmul(banks[tt][:, 0:512], lhsT, rhs,
                                                   start=st[tt], stop=False,
                                                   skip_group_check=True)
                                    st[tt] = False
                    return ins
                waits = [e_sh, e_sl] + xw
                if dcg == 0:
                    waits += [gslot[0][2], gslot[1][2]]
                e_mm_last = b.op("t", conv_mms, waits=waits, inc="T")
                cw_rel[csl] = e_mm_last
                if k + NCW < NGRP:
                    load_group(k + NCW)
            if 4 <= oc < 12:
                for j in range(2):
                    kc = (oc - 4) * 2 + j
                    wsl, wbuf, wrel = wt_stA.take()
                    e_wl = b.dma("g", f"d_wA{wsl}", wbuf[:],
                                 fw_h[kc * 128:(kc + 1) * 128, :], waits=[wrel])
                    e_wr = b.op("v", lambda E, wbuf=wbuf, kc=kc: E.reduce_max(
                        wmax[:, kc:kc + 1], wbuf[:], axis=AX.X,
                        apply_absolute_value=True), waits=[e_wl])
                    wt_stA.release(wsl, e_wr)
                    wsred_evs.append(e_wr)
            for tt in range(2):
                s_, bank_, _ = gslot[tt]
                tsl, tmp, trel = qkv_tmp.take()
                e_e1 = b.op("v", lambda E, p=bank_, tmp=tmp, oc=oc:
                            E.tensor_scalar_add(tmp[:], p[:, 0:512],
                                                cb_s[:, oc:oc + 1]),
                            waits=[e_mm_last, trel, e_bias_dma])
                pf_rot.release(s_, e_e1)
                hsl, bh, hrel = qkv_hi.take()
                lsl, bl_, lrel = qkv_lo.take()
                e_h = b.op("v", lambda E, tmp=tmp, bh=bh: E.tensor_copy(
                    bh[:], tmp[:]), waits=[e_e1, hrel])
                e_l = b.op("g", lambda E, tmp=tmp, bh=bh, bl_=bl_:
                           E.tensor_tensor(bl_[:], tmp[:], bh[:], ALU.subtract),
                           waits=[e_h, lrel])
                qkv_tmp.release(tsl, e_l)
                eh_d = b.dma("g", f"d_qd{hsl}",
                             qkvh_scr[oc, :, tt * 512:(tt + 1) * 512], bh[:],
                             waits=[e_h])
                el_d = b.dma("g", f"d_ql{lsl}",
                             qkvl_scr[oc, :, tt * 512:(tt + 1) * 512], bl_[:],
                             waits=[e_l])
                qkv_hi.release(hsl, eh_d)
                qkv_lo.release(lsl, el_d)

        e_wm = b.op("v", lambda E: E.reduce_max(red2[:], wmax[:], axis=AX.X),
                    waits=wsred_evs, inc="V")
        sl2, bank2, rl2 = pf_rot.take()
        e_tp2 = b.op("t", lambda E, p=bank2: E.transpose(
            p[0:1, 0:128], red2[:], idn[:]),
            waits=[e_wm, rl2, e_id], inc="T")

        b.op("v", lambda E, p=bank2: E.reduce_max(
            t_row[0:1, 0:1], p[0:1, 0:128], axis=AX.X), waits=[e_tp2])
        e_ws = b.op("v", lambda E: E.tensor_scalar_max(
            ws_t[:], t_row[0:1, 0:1], 1e-6))
        pf_rot.release(sl2, e_ws)
        slw, bankw, rlw = pf_rot.take()
        e_bw = b.op("t", lambda E, p=bankw: E.matmul(
            p[:, 0:1], ones_row[0:1, :], ws_t[0:1, 0:1], start=True, stop=True,
            skip_group_check=True), waits=[e_ws, rlw, e_masks], inc="T")

        b.op("v", lambda E, p=bankw: E.tensor_copy(ws_bc[:], p[:, 0:1]),
             waits=[e_bw])
        e_rws = b.op("v", lambda E: E.reciprocal(rws[:], ws_bc[:]))
        pf_rot.release(slw, e_rws)


        conv_done = b.barrier_evs() + b.dma_evs()
        stkA.close()

        stkB = ExitStack()
        ecB = stkB.enter_context
        qin_h = [ecB(nc.sbuf_tensor(f"qinh{i}", [128, 512], BF16)) for i in range(NWC)]
        qin_l = [ecB(nc.sbuf_tensor(f"qinl{i}", [128, 512], BF16)) for i in range(NWC)]
        kin_h = [ecB(nc.sbuf_tensor(f"kinh{i}", [128, 512], BF16)) for i in range(NWC)]
        kin_l = [ecB(nc.sbuf_tensor(f"kinl{i}", [128, 512], BF16)) for i in range(NWC)]
        vin_h = [ecB(nc.sbuf_tensor(f"vinh{i}", [128, 512], BF16)) for i in range(NWC)]
        vin_l = [ecB(nc.sbuf_tensor(f"vinl{i}", [128, 512], BF16)) for i in range(NWC)]
        wsf = Rot([ecB(nc.sbuf_tensor(f"wsf{i}", [128, 512], F32)) for i in range(1)])
        wts = {}
        for nm in ("wq", "wk", "wv"):
            wts[nm] = ([ecB(nc.sbuf_tensor(f"{nm}h{i}", [128, 512], BF16))
                        for i in range(NWC)],
                       [ecB(nc.sbuf_tensor(f"{nm}l{i}", [128, 512], BF16))
                        for i in range(NWC)])
        wts["wo"] = wts["wq"]
        pwf = Rot([ecB(nc.sbuf_tensor(f"pwf{i}", [128, 1024], F32))
                   for i in range(1)])
        pw_hi = [ecB(nc.sbuf_tensor(f"pwh{i}", [128, 1024], BF16))
                 for i in range(NWC)]
        pw_lo = [ecB(nc.sbuf_tensor(f"pwl{i}", [128, 1024], BF16))
                 for i in range(NWC)]
        qh_hi = [ecB(nc.sbuf_tensor(f"qhh{i}", [128, 512], BF16)) for i in range(NWC)]
        qh_lo = [ecB(nc.sbuf_tensor(f"qhl{i}", [128, 512], BF16)) for i in range(NWC)]
        kh_hi = [ecB(nc.sbuf_tensor(f"khh{i}", [128, 512], BF16)) for i in range(NWC)]
        kh_lo = [ecB(nc.sbuf_tensor(f"khl{i}", [128, 512], BF16)) for i in range(NWC)]
        vt_hi = [ecB(nc.sbuf_tensor(f"vth{i}", [128, 512], BF16)) for i in range(NWC)]
        vt_lo = [ecB(nc.sbuf_tensor(f"vtl{i}", [128, 512], BF16)) for i in range(NWC)]
        hp_tmp = Rot([ecB(nc.sbuf_tensor(f"hpt{i}", [128, 512], F32))
                      for i in range(1)])
        mskd = Rot([ecB(nc.sbuf_tensor(f"mskd{i}", [128, 512], F32))
                    for i in range(1)])
        exr = Rot([ecB(nc.sbuf_tensor(f"exr{i}", [128, 512], F32))
                   for i in range(1)])
        pnr = Rot([ecB(nc.sbuf_tensor(f"pnr{i}", [128, 512], F32))
                   for i in range(2)])
        phil = Rot([(ecB(nc.sbuf_tensor(f"philh{i}", [128, 512], BF16)),
                     ecB(nc.sbuf_tensor(f"phill{i}", [128, 512], BF16)))
                    for i in range(2)])
        hstage = [ecB(nc.sbuf_tensor(f"hstg{h}", [128, 2048], BF16))
                  for h in range(2)]
        lstage = [ecB(nc.sbuf_tensor(f"lstg{h}", [128, 2048], BF16))
                  for h in range(2)]
        ctx_hi = [ecB(nc.sbuf_tensor(f"cxh{i}", [128, 512], BF16))
                  for i in range(NPAIR)]
        ctx_lo = [ecB(nc.sbuf_tensor(f"cxl{i}", [128, 512], BF16))
                  for i in range(NPAIR)]
        mha_hi = [ecB(nc.sbuf_tensor(f"mhh{i}", [128, 512], BF16))
                  for i in range(NWC)]
        mha_lo = [ecB(nc.sbuf_tensor(f"mhl{i}", [128, 512], BF16))
                  for i in range(NWC)]
        xres = Rot([ecB(nc.sbuf_tensor(f"xres{i}", [128, 512], BF16))
                    for i in range(4)])

        rel_qin = [None] * NWC
        rel_kin = [None] * NWC
        rel_vin = [None] * NWC
        rel_w = {nm: [None] * NWC for nm in ("wq", "wk", "wv", "wo")}
        rel_pw = [None] * NWC
        rel_qh = [None] * NWC
        rel_kh = [None] * NWC
        rel_vt = [None] * NWC
        rel_ctx = [None] * NPAIR
        rel_mha = [None] * NWC
        stage_rel = [None, None]
        rel_x1s = [None] * NDC
        x1_out_ev = [[None] * NDC, [None] * NDC]
        x1_apply_ev = [[None] * NDC, [None] * NDC]
        nmx_i = [0]

        def split_f32(src, dst_h, dst_l, e_src, rels=(None, None), lo_eng="v"):
            e_h = b.op("v", lambda E, src=src, dst_h=dst_h: E.tensor_copy(
                dst_h[:], src[:]), waits=[e_src, rels[0]], inc="V")
            e_l = b.op(lo_eng, lambda E, src=src, dst_h=dst_h, dst_l=dst_l:
                       E.tensor_tensor(dst_l[:], src[:], dst_h[:], ALU.subtract),
                       waits=[e_h, rels[1]], inc=ENG_SEM[lo_eng])
            return e_h, e_l

        def emit_ln(src_tiles, apply_dst, src_evs, post_apply=None):
            e_stat = None
            for dc in range(NDC):
                ssl, sqt, srel = sqr.take()
                e_sq = b.op("v" if dc % 2 == 0 else "g",
                            lambda E, sqt=sqt, dc=dc, src=src_tiles:
                            E.tensor_mul(sqt[:], src[dc][:], src[dc][:]),
                            waits=[src_evs[dc], srel])

                def stat_mm(E, dc=dc, sqt=sqt, src=src_tiles):
                    E.matmul(PS0[0:1, :], ones_col[:, 0:1], src[dc][:],
                             start=(dc == 0), stop=(dc == NDC - 1),
                             skip_group_check=True)
                    return E.matmul(PS1[0:1, :], ones_col[:, 0:1], sqt[:],
                                    start=(dc == 0), stop=(dc == NDC - 1),
                                    skip_group_check=True)
                e_stat = b.op("t", stat_mm,
                              waits=[e_sq, src_evs[dc], ps_rel[0], e_masks],
                              inc="T")
                sqr.release(ssl, e_stat)
            inv_d = 1.0 / D
            b.op("v", lambda E: E.tensor_scalar_mul(m_row[:], PS0[0:1, :], inv_d),
                 waits=[e_stat])
            b.op("v", lambda E: E.tensor_scalar_mul(v_row[:], PS1[0:1, :], inv_d))
            b.op("v", lambda E: E.tensor_mul(t_row[:], m_row[:], m_row[:]))
            b.op("v", lambda E: E.tensor_sub(v_row[:], v_row[:], t_row[:]))
            e_r1 = b.op("v", lambda E: E.tensor_scalar_add(v_row[:], v_row[:],
                                                           EPS_LN))
            ps_rel[0] = e_r1
            e_sq2 = b.op("a", lambda E: E.activation(u_row[:], v_row[:], ACTF.Sqrt),
                         waits=[e_r1], inc="A")
            b.op("v", lambda E: E.reciprocal(i_row[:], u_row[:]), waits=[e_sq2])
            for _ in range(2):
                b.op("v", lambda E: E.tensor_mul(t_row[:], i_row[:], i_row[:]))
                b.op("v", lambda E: E.tensor_mul(t_row[:], t_row[:], v_row[:]))
                b.op("v", lambda E: E.tensor_scalar(t_row[:], t_row[:], -0.5, 1.5,
                                                    ALU.mult, ALU.add))
                e_r2 = b.op("v", lambda E: E.tensor_mul(i_row[:], i_row[:],
                                                        t_row[:]))
            sl_m, bank_m, rel_m = pf_rot.take()
            e_bm = b.op("t", lambda E, p=bank_m: E.matmul(
                p[:, 0:512], ones_row[0:1, :], m_row[0:1, :], start=True,
                stop=True, skip_group_check=True),
                waits=[e_r1, rel_m, e_masks], inc="T")
            sl_i, bank_i, rel_i = pf_rot.take()
            e_bi = b.op("t", lambda E, p=bank_i: E.matmul(
                p[:, 0:512], ones_row[0:1, :], i_row[0:1, :], start=True,
                stop=True, skip_group_check=True),
                waits=[e_r2, rel_i, e_masks], inc="T")
            e_cm = b.op("v", lambda E, p=bank_m: E.tensor_copy(
                m_bc_s[:], p[:, 0:512]), waits=[e_bm])
            e_ci = b.op("v", lambda E, p=bank_i: E.tensor_copy(
                i_bc_s[:], p[:, 0:512]), waits=[e_bi])
            pf_rot.release(sl_m, e_cm)
            pf_rot.release(sl_i, e_ci)
            apply_evs = []
            for dc in range(NDC):
                dst, wa = apply_dst(dc)
                eng = "v" if dc % 2 == 1 else "g"
                b.op(eng, lambda E, dc=dc, dst=dst, src=src_tiles:
                     E.tensor_tensor(dst[:], src[dc][:], m_bc_s[:],
                                     ALU.subtract),
                     waits=[e_cm, e_ci, src_evs[dc]] + list(wa))
                e_ap = b.op(eng, lambda E, dst=dst:
                            E.tensor_tensor(dst[:], dst[:], i_bc_s[:],
                                            ALU.mult))
                apply_evs.append(e_ap)
                if post_apply is not None:
                    post_apply(dc, e_ap)
            return apply_evs

        for blk in range(NB):
            c0 = blk * 512
            bw = conv_done if blk == 0 else []
            ein_q, ein_k, ein_v = [], [], []
            for wc in range(NWC):
                for (dsth, dstl, base, ein, relv) in (
                        (qin_h, qin_l, 0, ein_q, rel_qin),
                        (kin_h, kin_l, 4, ein_k, rel_kin),
                        (vin_h, vin_l, 8, ein_v, rel_vin)):
                    sname = {0: "d_qin", 4: "d_kin", 8: "d_vin"}[base]
                    e1 = b.dma("a", sname, dsth[wc][:],
                               qkvh_scr[base + wc, :, c0:c0 + 512],
                               waits=bw + [relv[wc]])
                    e2 = b.dma("a", sname, dstl[wc][:],
                               qkvl_scr[base + wc, :, c0:c0 + 512],
                               waits=[relv[wc]])
                    ein.append((e1, e2))
            qin_all = b.ev("d_qin")
            kin_all = b.ev("d_kin")
            vin_all = b.ev("d_vin")
            ein_q = [(qin_all, qin_all)] * NWC
            ein_k = [(kin_all, kin_all)] * NWC
            ein_v = [(vin_all, vin_all)] * NWC
            wsplit_ev = {}
            for name, h_ap in (("wq", wq_h), ("wk", wk_h), ("wv", wv_h)):
                dsth, dstl = wts[name]
                evs = []
                for wc in range(NWC):
                    ssl, sbuf, srel = wsf.take()
                    e_w = b.dma("a", "d_wsf", sbuf[:],
                                h_ap[wc * 128:(wc + 1) * 128, :],
                                waits=bw + [srel, rel_w[name][wc]])
                    e_h, e_l = split_f32(sbuf, dsth[wc], dstl[wc], e_w,
                                         rels=(rel_w[name][wc],
                                               rel_w[name][wc]),
                                         lo_eng="v" if wc % 2 == 0 else "g")
                    wsf.release(ssl, e_l)
                    evs.append((e_h, e_l))
                wsplit_ev[name] = evs

            def hp_group(dst_hi, dst_lo, in_h, in_l, ein, wname, bias_kind,
                         rel_dst, vt_mode=False):
                whi, wlo = wts[wname]
                out_evs = []
                for i in range(NWC):
                    psl, pbank, prel = pf_rot.take()

                    def mms(E, i=i, pbank=pbank, in_h=tuple(in_h),
                            in_l=tuple(in_l), whi=tuple(whi), wlo=tuple(wlo),
                            vt_mode=vt_mode):
                        ins = None
                        st = True
                        for wc in range(NWC):
                            if vt_mode:
                                l_h = in_h[wc][:, i * 128:(i + 1) * 128]
                                l_l = in_l[wc][:, i * 128:(i + 1) * 128]
                                r_h, r_l = whi[wc][:], wlo[wc][:]
                            else:
                                l_h = whi[wc][:, i * 128:(i + 1) * 128]
                                l_l = wlo[wc][:, i * 128:(i + 1) * 128]
                                r_h, r_l = in_h[wc][:], in_l[wc][:]
                            for lhsT, rhs in ((l_h, r_h), (l_l, r_h),
                                              (l_h, r_l)):
                                ins = E.matmul(pbank[:, 0:512], lhsT, rhs,
                                               start=st, stop=False,
                                               skip_group_check=True)
                                st = False
                        return ins
                    waits = [prel]
                    for wc in range(NWC):
                        waits += [ein[wc][0], ein[wc][1],
                                  wsplit_ev[wname][wc][0],
                                  wsplit_ev[wname][wc][1]]
                    e_mm = b.op("t", mms, waits=waits, inc="T")
                    tsl, tmp, trel = hp_tmp.take()
                    if bias_kind == "q":
                        e_e = b.op("v", lambda E, p=pbank, tmp=tmp, i=i:
                                   E.tensor_scalar(tmp[:], p[:, 0:512], 0.125,
                                                   bqs_s[:, i:i + 1],
                                                   ALU.mult, ALU.add),
                                   waits=[e_mm, trel, e_bqs], inc="V")
                    elif bias_kind == "k":
                        e_e = b.op("a", lambda E, p=pbank, tmp=tmp, i=i: E.add(
                            tmp[:], p[:, 0:512], bk_s[:, i:i + 1]),
                            waits=[e_mm, trel, e_bias_dma], inc="A")
                    else:
                        e_e = b.op("v", lambda E, p=pbank, tmp=tmp:
                                   E.tensor_tensor(tmp[:], p[:, 0:512],
                                                   bv_bc[:], ALU.add),
                                   waits=[e_mm, trel, e_bvbc], inc="V")
                    pf_rot.release(psl, e_e)
                    e_h, e_l = split_f32(tmp, dst_hi[i], dst_lo[i], e_e,
                                         rels=(rel_dst[i], rel_dst[i]),
                                         lo_eng="v" if i % 2 == 0 else "g")
                    hp_tmp.release(tsl, e_l)
                    out_evs.append((e_h, e_l))
                return out_evs
            eqh = hp_group(qh_hi, qh_lo, qin_h, qin_l, ein_q, "wq", "q", rel_qh)
            ekh = hp_group(kh_hi, kh_lo, kin_h, kin_l, ein_k, "wk", "k", rel_kh)
            evt = hp_group(vt_hi, vt_lo, vin_h, vin_l, ein_v, "wv", "v", rel_vt,
                           vt_mode=True)
            for wc in range(NWC):
                rel_qin[wc] = eqh[-1][1]
                rel_kin[wc] = ekh[-1][1]
                rel_vin[wc] = evt[-1][1]
            qh_ready = [e for pr in eqh for e in pr]
            kh_ready = [e for pr in ekh for e in pr]
            vt_ready = [e for pr in evt for e in pr]

            dsth, dstl = wts["wo"]
            evs = []
            for wc in range(NWC):
                ssl, sbuf, srel = wsf.take()
                e_w = b.dma("a", "d_wsf", sbuf[:],
                            wo_h[wc * 128:(wc + 1) * 128, :],
                            waits=[srel, rel_w["wo"][wc]] + qh_ready)
                e_h, e_l = split_f32(sbuf, dsth[wc], dstl[wc], e_w,
                                     rels=(rel_w["wo"][wc],
                                           rel_w["wo"][wc]),
                                     lo_eng="v" if wc % 2 == 0 else "g")
                wsf.release(ssl, e_l)
                evs.append((e_h, e_l))
            wsplit_ev["wo"] = evs

            def emit_pw_stream(half):
                evs = []
                for wc in range(NWC):
                    fsl, fbuf, frel = pwf.take()
                    e_w = b.dma("s", ("d_pwf0", "d_pwf1")[fsl], fbuf[:],
                                pw_h[wc * 128:(wc + 1) * 128,
                                     half * 1024:(half + 1) * 1024],
                                waits=[frel] + (bw if blk == 0 else []))
                    e_h = b.op("a", lambda E, fbuf=fbuf, wc=wc: E.copy(
                        pw_hi[wc][:], fbuf[:]),
                        waits=[e_w, rel_pw[wc]], inc="A")
                    e_l = b.op("v", lambda E, fbuf=fbuf, wc=wc: E.tensor_tensor(
                        pw_lo[wc][:], fbuf[:], pw_hi[wc][:], ALU.subtract),
                        waits=[e_h], inc="V")
                    pwf.release(fsl, e_l)
                    evs += [e_h, e_l]
                return evs
            pw_stream_evs = {}
            pw_stream_evs[0] = emit_pw_stream(0)

            sc_rot = Rot(PF)
            sc_rot.rel = list(pf_rot.rel[0:3])

            def emit_chain_smm(p, qt, hh):
                psl, pbank, prel = sc_rot.take()
                r0 = hh * 64

                def smm(E, p_=p, qt=qt, r0=r0, pbank=pbank):
                    ins = None
                    st = True
                    qsl = slice(qt * 128, (qt + 1) * 128)
                    for lhsT, rhs in (
                            (qh_hi[p_][r0:r0 + 64, qsl],
                             kh_hi[p_][r0:r0 + 64, :]),
                            (qh_lo[p_][r0:r0 + 64, qsl],
                             kh_hi[p_][r0:r0 + 64, :]),
                            (qh_hi[p_][r0:r0 + 64, qsl],
                             kh_lo[p_][r0:r0 + 64, :])):
                        ins = E.matmul(pbank[:, 0:512], lhsT, rhs,
                                       start=st, stop=False,
                                       skip_group_check=True,
                                       tile_position=(r0, 0))
                        st = False
                    return ins
                e_mm = b.op("t", smm, waits=[prel] + qh_ready + kh_ready,
                            inc="T")
                return (qt, hh, psl, pbank, e_mm)

            def emit_sm1(ch):
                qt, hh, psl, pbank, e_mm = ch
                msl, mbuf, mrel = mskd.take()
                ni = nmx_i[0] % 16
                nmx_i[0] += 1
                b.op("v", lambda E, pbank=pbank, mbuf=mbuf, qt=qt:
                     E.tensor_tensor(mbuf[:], pbank[:, 0:512],
                                     amask[qt][:], ALU.add),
                     waits=[e_mm, mrel, e_masks])
                e_msk = b.op("v", lambda E, mbuf=mbuf, ni=ni:
                             E.reduce_max(nmx_a[:, ni:ni + 1], mbuf[:],
                                          axis=AX.X, negate=True))
                sc_rot.release(psl, e_msk)
                return (qt, hh, msl, mbuf, ni, e_msk)

            def emit_sm2(s1):
                qt, hh, msl, mbuf, ni, e_msk = s1
                nmx = nmx_a[:, ni:ni + 1]
                sums = sum_a[:, ni:ni + 1]
                rec = rec_a[:, ni:ni + 1]
                esl, ebuf, erel = exr.take()
                e_exp = b.op("a", lambda E, mbuf=mbuf, ebuf=ebuf,
                             nmx=nmx, sums=sums: E.activation(
                                 ebuf[:], mbuf[:], ACTF.Exp, bias=nmx,
                                 scale=1.0, accum_out=sums),
                             waits=[e_msk, erel], inc="A")
                mskd.release(msl, e_exp)
                e_rc = b.op("v", lambda E, rec=rec, sums=sums:
                            E.reciprocal(rec, sums), waits=[e_exp])
                nsl, pnb, nrel = pnr.take()
                e_pm = b.op("a", lambda E, ebuf=ebuf, rec=rec, pnb=pnb:
                            E.mul(pnb[:], ebuf[:], rec),
                            waits=[e_rc, nrel], inc="A")
                exr.release(esl, e_pm)
                fsl, (hbuf, lbuf), frel = phil.take()
                e_hi = b.op("v", lambda E, pnb=pnb, hbuf=hbuf:
                            E.tensor_copy(hbuf[:], pnb[:]),
                            waits=[e_pm, frel])
                e_lo = b.op("g", lambda E, pnb=pnb, hbuf=hbuf, lbuf=lbuf:
                            E.tensor_tensor(lbuf[:], pnb[:], hbuf[:],
                                            ALU.subtract),
                            waits=[e_hi])
                pnr.release(nsl, e_lo)
                return (qt, hh, fsl, hbuf, lbuf, e_hi, e_lo)

            def stage_ap(stg, qt):
                full = stg[:]
                return bass.AP(tensor=full.tensor, offset=qt * 128,
                               ap=[[2048, 128], [512, 4], [1, 128]])

            def emit_chain_tp(sm, evac_evs):
                qt, hh, fsl, hbuf, lbuf, e_hi, e_lo = sm

                def tp_fn(E, hbuf=hbuf, lbuf=lbuf):
                    ins = None
                    for sc in range(4):
                        E.transpose(PT0[:, sc * 128:(sc + 1) * 128],
                                    hbuf[:, sc * 128:(sc + 1) * 128],
                                    idnb[:])
                        ins = E.transpose(PT1[:, sc * 128:(sc + 1) * 128],
                                          lbuf[:, sc * 128:(sc + 1) * 128],
                                          idnb[:])
                    return ins
                e_tp = b.op("t", tp_fn,
                            waits=[e_hi, e_lo] + pt_rel[0] + [e_idnb],
                            inc="T")
                phil.release(fsl, e_tp)
                e_ph = b.op("a", lambda E, hh=hh, qt=qt: E.copy(
                    stage_ap(hstage[hh], qt), PT0[:, 0:512]),
                    waits=[e_tp, stage_rel[hh]], inc="A")
                e_pl = b.op("v", lambda E, hh=hh, qt=qt: E.tensor_copy(
                    stage_ap(lstage[hh], qt), PT1[:, 0:512]),
                    waits=[e_tp, stage_rel[hh]])
                pt_rel[0] = [e_ph, e_pl]
                evac_evs[hh] += [e_ph, e_pl]

            ctxb_rel = [pf_rot.rel[3]]

            def emit_ctx_group(p, hh, evac_ws):
                e_last = None
                vcol = p * 128 + hh * 64
                for sc in range(4):
                    def ctx_fn(E, hh=hh, sc=sc, vcol=vcol):
                        ins = None
                        st = (sc == 0)
                        for lhsT, rhs in (
                                (vt_hi[sc][:, vcol:vcol + 64],
                                 hstage[hh][:, sc * 512:(sc + 1) * 512]),
                                (vt_hi[sc][:, vcol:vcol + 64],
                                 lstage[hh][:, sc * 512:(sc + 1) * 512]),
                                (vt_lo[sc][:, vcol:vcol + 64],
                                 hstage[hh][:, sc * 512:(sc + 1) * 512])):
                            ins = E.matmul(
                                CTXB[hh * 64:(hh + 1) * 64, 0:512],
                                lhsT, rhs, start=st, stop=False,
                                skip_group_check=True,
                                tile_position=(0, hh * 64))
                            st = False
                        return ins
                    waits = list(evac_ws) + vt_ready
                    if sc == 0 and hh == 0 and ctxb_rel[0] is not None:
                        waits.append(ctxb_rel[0])
                    e_last = b.op("t", ctx_fn, waits=waits, inc="T")
                stage_rel[hh] = e_last
                return e_last

            def emit_cevac(p, e_ctx_last):
                e_ch = b.op("a", lambda E, p=p: E.copy(
                    ctx_hi[p][:], CTXB[:, 0:512]),
                    waits=[e_ctx_last, rel_ctx[p]], inc="A")
                e_cl = b.op("v", lambda E, p=p: E.tensor_tensor(
                    ctx_lo[p][:], CTXB[:, 0:512], ctx_hi[p][:],
                    ALU.subtract), waits=[e_ch], inc="V")
                ctxb_rel[0] = e_cl
                rel_ctx[p] = e_cl

            mskd.items.append(hp_tmp.items[0])
            mskd.rel.append(hp_tmp.rel[0])
            chains = [(p, hh, qt) for p in range(NPAIR) for hh in range(2)
                      for qt in range(4)]
            evac_map = {(p, hh): [] for p in range(NPAIR) for hh in range(2)}
            sm1q = []
            sm2q = []
            for i in range(len(chains) + 2):
                main = i < len(chains)
                if main:
                    p, hh, qt = chains[i]
                    ch = emit_chain_smm(p, qt, hh)
                    sm1q.append((p, hh, qt, emit_sm1(ch)))
                if len(sm1q) > (1 if main else 0):
                    p1, hh1, qt1, s1 = sm1q.pop(0)
                    sm2q.append((p1, hh1, qt1, emit_sm2(s1)))
                if len(sm2q) > (1 if main else 0):
                    p2, hh2, qt2, sm = sm2q.pop(0)
                    emit_chain_tp(sm, [evac_map[(p2, 0)],
                                       evac_map[(p2, 1)]])
                    if qt2 == 3:
                        e_g = emit_ctx_group(p2, hh2, evac_map[(p2, hh2)])
                        if hh2 == 1:
                            emit_cevac(p2, e_g)
            hp_tmp.rel[0] = mskd.rel[1]
            mskd.items.pop()
            mskd.rel.pop()
            pf_rot.rel[0:3] = sc_rot.rel
            pf_rot.rel[3] = ctxb_rel[0]

            ctx_ready = [rel_ctx[p] for p in range(NPAIR)]

            mha_evs = []
            woh, wol = wts["wo"]
            for oc in range(NWC):
                psl, pbank, prel = pf_rot.take()

                def womm(E, oc=oc, pbank=pbank, woh=tuple(woh), wol=tuple(wol)):
                    ins = None
                    st = True
                    for p in range(NPAIR):
                        osl = slice(oc * 128, (oc + 1) * 128)
                        for lhsT, rhs in ((woh[p][:, osl], ctx_hi[p][:]),
                                          (wol[p][:, osl], ctx_hi[p][:]),
                                          (woh[p][:, osl], ctx_lo[p][:])):
                            ins = E.matmul(pbank[:, 0:512], lhsT, rhs,
                                           start=st, stop=False,
                                           skip_group_check=True)
                            st = False
                    return ins
                waits = [prel] + ctx_ready
                for p in range(NPAIR):
                    waits += [wsplit_ev["wo"][p][0], wsplit_ev["wo"][p][1]]
                e_mm = b.op("t", womm, waits=waits, inc="T")
                tsl, tmp, trel = hp_tmp.take()
                e_e = b.op("a", lambda E, pbank=pbank, tmp=tmp, oc=oc: E.add(
                    tmp[:], pbank[:, 0:512], bo_s[:, oc:oc + 1]),
                    waits=[e_mm, trel], inc="A")
                pf_rot.release(psl, e_e)
                e_h, e_l = split_f32(tmp, mha_hi[oc], mha_lo[oc], e_e,
                                     rels=(rel_mha[oc], rel_mha[oc]),
                                     lo_eng="v" if oc % 2 == 0 else "g")
                hp_tmp.release(tsl, e_l)
                mha_evs += [e_h, e_l]
            for p in range(NPAIR):
                rel_ctx[p] = mha_evs[-1]

            x1pre_evs = [None] * NDC
            for half in range(2):
                pw_evs_half = pw_stream_evs[half]
                e_mm_half = None
                for dc in range(half * 8, half * 8 + 8):
                    rsl_h, rbh, rrel_h = xres.take()
                    rsl_l, rbl, rrel_l = xres.take()
                    xo = dc * (TOK + 2) + 2 + c0
                    e_rh = b.dma("s", f"d_xr{rsl_h}", rbh[:],
                                 xh_scr[:, xo:xo + 512],
                                 waits=[rrel_h, e_xscr])
                    e_rl = b.dma("s", f"d_xr{rsl_l}", rbl[:],
                                 xl_scr[:, xo:xo + 512],
                                 waits=[rrel_l])
                    psl, pbank, prel = pf_rot.take()
                    lcol = dc * 128 - half * 1024

                    def pjmm(E, lcol=lcol, pbank=pbank, rbh=rbh, rbl=rbl):
                        ins = None
                        st = True
                        dsl = slice(lcol, lcol + 128)
                        for wc in range(NWC):
                            for lhsT, rhs in (
                                    (pw_hi[wc][:, dsl], mha_hi[wc][:]),
                                    (pw_lo[wc][:, dsl], mha_hi[wc][:]),
                                    (pw_hi[wc][:, dsl], mha_lo[wc][:])):
                                ins = E.matmul(pbank[:, 0:512], lhsT, rhs,
                                               start=st, stop=False,
                                               skip_group_check=True)
                                st = False
                        E.matmul(pbank[:, 0:512], idnb[:], rbh[:], start=False,
                                 stop=False, skip_group_check=True)
                        return E.matmul(pbank[:, 0:512], idnb[:], rbl[:],
                                        start=False, stop=True,
                                        skip_group_check=True)
                    waits = [prel, e_rh, e_rl, e_idnb] + mha_evs + pw_evs_half
                    e_mm = b.op("t", pjmm, waits=waits, inc="T")
                    e_mm_half = e_mm
                    xres.release(rsl_h, e_mm)
                    xres.release(rsl_l, e_mm)
                    e_e = b.op("a", lambda E, pbank=pbank, dc=dc: E.add(
                        x1s[dc][:], pbank[:, 0:512], pb_s[:, dc:dc + 1]),
                        waits=[e_mm, rel_x1s[dc]], inc="A")
                    pf_rot.release(psl, e_e)
                    x1pre_evs[dc] = e_e
                for wc in range(NWC):
                    rel_pw[wc] = e_mm_half
                if half == 0:
                    pw_stream_evs[1] = emit_pw_stream(1)
            for wc in range(NWC):
                rel_mha[wc] = x1pre_evs[-1]
                rel_qh[wc] = x1pre_evs[-1]
                rel_kh[wc] = x1pre_evs[-1]
                rel_vt[wc] = x1pre_evs[-1]
                for nm in ("wq", "wk", "wv", "wo"):
                    rel_w[nm][wc] = x1pre_evs[-1]

            apply_evs = emit_ln(x1s, lambda dc: (x1s[dc], []), x1pre_evs)
            for dc in range(NDC):
                if blk == 0:
                    e_mx = b.op("v", lambda E, dc=dc: E.reduce_max(
                        chmax[:, dc:dc + 1], x1s[dc][:], axis=AX.X,
                        apply_absolute_value=True),
                        waits=[apply_evs[dc]], inc="V")
                else:
                    b.op("v", lambda E, dc=dc: E.reduce_max(
                        red1[:], x1s[dc][:], axis=AX.X,
                        apply_absolute_value=True), waits=[apply_evs[dc]])
                    e_mx = b.op("v", lambda E, dc=dc: E.tensor_max(
                        chmax[:, dc:dc + 1], chmax[:, dc:dc + 1], red1[:]))
                x1_apply_ev[blk][dc] = e_mx
                e_d = b.dma("g", "d_x1o", x1_scr[dc, :, c0:c0 + 512],
                            x1s[dc][:], waits=[apply_evs[dc]])
                x1_out_ev[blk][dc] = e_d
            x1o_all = b.ev("d_x1o")
            for dc in range(NDC):
                x1_out_ev[blk][dc] = x1o_all
                if blk == 0:
                    rel_x1s[dc] = x1o_all

        attn_done = b.barrier_evs() + b.dma_evs()
        stkB.close()

        stkC = ExitStack()
        ecC = stkC.enter_context
        wt_st = Rot([ecC(nc.sbuf_tensor(f"wtst{i}", [128, 2048], BF16))
                     for i in range(2)])
        wq8 = [ecC(nc.sbuf_tensor(f"wq8_{i}", [128, 2048], F8))
               for i in range(NDC)]
        xq8_2 = [[ecC(nc.sbuf_tensor(f"xq8_{s}_{i}", [128, 512], F8))
                  for i in range(NDC)] for s in range(2)]
        yr = Rot([ecC(nc.sbuf_tensor(f"y{i}", [128, 512], BF16))
                  for i in range(6)])
        xstg = Rot([ecC(nc.sbuf_tensor(f"xstg{i}", [128, 512], F32))
                    for i in range(2)])
        outr = Rot([ecC(nc.sbuf_tensor(f"or{i}", [128, 512], F32))
                    for i in range(4)])

        e_lm = b.op("v", lambda E: E.reduce_max(red1[:], chmax[:], axis=AX.X),
                    waits=[x1_apply_ev[1][dc] for dc in range(NDC)], inc="V")
        sl1, bank1, rl1 = pf_rot.take()
        e_tp1 = b.op("t", lambda E, p=bank1: E.transpose(
            p[0:1, 0:128], red1[:], idn[:]),
            waits=[e_lm, rl1, e_id], inc="T")
        e_xm = b.op("v", lambda E, p=bank1: E.reduce_max(
            cc_s[0:1, 0:1], p[0:1, 0:128], axis=AX.X),
            waits=[e_tp1, e_masks], inc="V")
        pf_rot.release(sl1, e_xm)
        e_ccw = b.dma("g", "d_ccd", cc_in.ap()[:, :], cc_s[:], waits=[e_xm])
        e_cc = b.op("g", lambda E: E.collective_compute(
            "AllReduce", ALU.max, replica_groups=[list(range(NCORE))],
            ins=[cc_in.ap().opt()], outs=[cc_out.ap().opt()]),
            waits=[e_ccw], inc="C")
        e_ccr = b.dma("g", "d_ccd", cc_s[:], cc_out.ap()[:, :], waits=[e_cc])
        wq8_evs = []
        for kc in range(NDC):
            ssl, sbuf, srel = wt_st.take()
            e_w = b.dma("s", f"d_wt{ssl}", sbuf[:],
                        fw_h[:, kc * 128:(kc + 1) * 128],
                        waits=attn_done + [srel], transpose=True)
            e_q = b.op("v", lambda E, sbuf=sbuf, kc=kc: E.tensor_scalar(
                wq8[kc][:], sbuf[:], rws[:, 0:1], None, ALU.mult),
                waits=[e_w, e_rws])
            wt_st.release(ssl, e_q)
            wq8_evs.append(e_q)

        e_xs = b.op("v", lambda E: E.tensor_scalar_max(
            xs_t[:], cc_s[0:1, 0:1], 1e-6), waits=[e_ccr], inc="V")
        slx, bankx, rlx = pf_rot.take()
        e_bx = b.op("t", lambda E, p=bankx: E.matmul(
            p[:, 0:1], ones_row[0:1, :], xs_t[0:1, 0:1], start=True, stop=True,
            skip_group_check=True), waits=[e_xs, rlx, e_masks], inc="T")

        b.op("v", lambda E, p=bankx: E.tensor_copy(xs_bc[:], p[:, 0:1]),
             waits=[e_bx])
        e_rxs = b.op("v", lambda E: E.reciprocal(rxs[:], xs_bc[:]))
        pf_rot.release(slx, e_rxs)

        e_sy = b.op("v", lambda E: E.tensor_mul(sy_bc[:], xs_bc[:], ws_bc[:]),
                    waits=[e_rws, e_rxs])

        xq0_evs = []
        for dc in range(NDC):
            xsl, xbuf2, xrel2 = xstg.take()
            e_r = b.dma("s", f"d_xs{xsl}", xbuf2[:],
                        x1_scr[dc, :, 0:512], waits=[xrel2])
            e_q = b.op("v", lambda E, dc=dc, xbuf2=xbuf2:
                       E.tensor_scalar(xq8_2[0][dc][:], xbuf2[:],
                                       rxs[:, 0:1], None, ALU.mult),
                       waits=[e_r, e_rxs], inc="V")
            xstg.release(xsl, e_q)
            xq0_evs.append(e_q)

        ffn_mm_done = [None]
        for tt in (1, 0):
            c0 = tt * 512
            xq8 = xq8_2[tt]
            xq_evs = []
            if tt == 1:
                x1_evs = [x1_out_ev[1][dc] for dc in range(NDC)]
                for dc in range(NDC):
                    e_q = b.op("v", lambda E, dc=dc, xq8=xq8: E.tensor_scalar(
                        xq8[dc][:], x1s[dc][:], rxs[:, 0:1], None, ALU.mult),
                        waits=[x1_evs[dc], e_rxs], inc="V")
                    xq_evs.append(e_q)
            else:
                xq_evs = xq0_evs
                for dc in range(NDC):
                    b.dma("s", "d_x1r", x1s[dc][:], x1_scr[dc, :, 0:512],
                          waits=[rel_x1s[dc]])
                x1_evs = [b.ev("d_x1r")] * NDC
            x2_evs = [None] * NDC
            for oc_ in range(NDC):
                psl, pbank, prel = pf_rot.take()

                def fmm(E, oc_=oc_, pbank=pbank, xq8=xq8):
                    ins = None
                    for kc in range(NDC):
                        ins = E.matmul(pbank[:, 0:512],
                                       wq8[kc][:, oc_ * 128:(oc_ + 1) * 128],
                                       xq8[kc][:], start=(kc == 0),
                                       stop=(kc == NDC - 1),
                                       skip_group_check=True)
                    return ins
                e_mm = b.op("t", fmm, waits=[prel] + xq_evs + wq8_evs, inc="T")
                ysl, ybuf, yrel = yr.take()
                y2sl, y2buf, y2rel = yr.take()
                e_y = b.op("v", lambda E, pbank=pbank, ybuf=ybuf:
                           E.tensor_scalar(ybuf[:], pbank[:, 0:512],
                                           sy_bc[:, 0:1], None, ALU.mult),
                           waits=[e_mm, yrel, e_sy], inc="V")
                pf_rot.release(psl, e_y)
                e_la = b.op("g", lambda E, ybuf=ybuf, y2buf=y2buf:
                            E.tensor_copy(y2buf[:].bitcast(U16),
                                          ybuf[:].bitcast(U16)),
                            waits=[e_y, y2rel])
                yr.release(ysl, e_la)

                e_yb = b.op("v", lambda E, y2buf=y2buf, oc_=oc_: E.tensor_scalar(
                    y2buf[:], y2buf[:], fb_s[:, oc_:oc_ + 1], None, ALU.add),
                    waits=[e_la, e_fb, xq_evs[oc_]])
                e_x2 = b.op("g", lambda E, y2buf=y2buf, oc_=oc_:
                            E.tensor_tensor(x1s[oc_][:], y2buf[:],
                                            x1s[oc_][:], ALU.add),
                            waits=[e_yb, x1_evs[oc_]])
                yr.release(y2sl, e_x2)
                x2_evs[oc_] = e_x2
            ffn_mm_done[0] = b.ev("T")
            out_slots = {}

            def apply_dst(dc):
                osl_, obuf, orel = outr.take()
                out_slots[dc] = (obuf, osl_)
                return obuf, [orel]

            def post_apply(dc, e_ap, c0=c0, tt=tt):
                obuf, osl_ = out_slots[dc]
                e_o = b.dma("g", f"d_ot{osl_}",
                            out_h[dc * 128:(dc + 1) * 128, c0:c0 + 512],
                            obuf[:], waits=[e_ap])
                outr.release(osl_, e_o)
                if tt == 1:
                    rel_x1s[dc] = e_ap
            ln2_evs = emit_ln(x1s, apply_dst, x2_evs, post_apply=post_apply)

        b.op("s", lambda E: E.nop(), waits=b.dma_evs())
        b.op("g", lambda E: E.nop(), waits=b.dma_evs())
        stkC.close()

        block = top.enter_context(nc.Block())

        @block.tensor
        def _(e):
            for f in b.q["t"]:
                f(e)

        @block.vector
        def _(e):
            for f in b.q["v"]:
                f(e)

        @block.scalar
        def _(e):
            for f in b.q["a"]:
                f(e)

        @block.gpsimd
        def _(e):
            for f in b.q["g"]:
                f(e)

        @block.sync
        def _(e):
            for f in b.q["s"]:
                f(e)

    return nc


_CACHE = {}


def kernel(**inputs):
    x = np.asarray(inputs["L0_attn_x"])[0]
    cw = np.ascontiguousarray(inputs["L0_attn_conv_w"])
    cb = np.ascontiguousarray(inputs["L0_attn_conv_b"])
    wq = np.ascontiguousarray(np.asarray(inputs["L0_attn_wq"]).reshape(W, W))
    wk = np.ascontiguousarray(np.asarray(inputs["L0_attn_wk"]).reshape(W, W))
    wv = np.ascontiguousarray(np.asarray(inputs["L0_attn_wv"]).reshape(W, W))
    bq = np.ascontiguousarray(np.asarray(inputs["L0_attn_bq"]).reshape(W))
    bk = np.ascontiguousarray(np.asarray(inputs["L0_attn_bk"]).reshape(W))
    bv = np.ascontiguousarray(np.asarray(inputs["L0_attn_bv"]).reshape(W))
    wo = np.ascontiguousarray(np.asarray(inputs["L0_attn_wo"]).reshape(W, W))
    bo = np.ascontiguousarray(inputs["L0_attn_bo"])
    pw = np.ascontiguousarray(inputs["L0_attn_proj_w"])
    pb = np.ascontiguousarray(inputs["L0_attn_proj_b"])
    fw = np.ascontiguousarray(inputs["L0_ffn_weight"])
    fb = np.ascontiguousarray(inputs["L0_ffn_bias"])

    if "nc" not in _CACHE:
        _CACHE["nc"] = build_program()
    nc = _CACHE["nc"]

    in_maps = []
    for c in range(NCORE):
        xs = np.zeros((TOK + 2, D), dtype=np.float32)
        lo = c * TOK
        xs[2:] = x[lo:lo + TOK]
        if c > 0:
            xs[0:2] = x[lo - 2:lo]
        in_maps.append({
            "x": xs, "conv_w": cw, "conv_b": cb,
            "wq": wq, "wk": wk, "wv": wv, "bq": bq, "bk": bk, "bv": bv,
            "wo": wo, "bo": bo, "proj_w": pw, "proj_b": pb,
            "ffn_w": fw, "ffn_b": fb,
        })
    res = run_bass_kernel_spmd(nc, in_maps, core_ids=list(range(NCORE)))
    full = np.concatenate([res.results[c]["out"] for c in range(NCORE)], axis=1)
    return np.ascontiguousarray(full.T)[None].astype(np.float32)
